# version 11
# baseline (speedup 1.0000x reference)
import numpy as np
import dataclasses
from contextlib import ExitStack
import concourse.bass as bass
import concourse.mybir as mybir

F32 = mybir.dt.float32
BF16 = mybir.dt.bfloat16
U8 = mybir.dt.uint8
AF = mybir.ActivationFunctionType
ALU = mybir.AluOpType
AX = mybir.AxisListType

S, E, H, D, FF, NE = 4096, 1024, 16, 64, 4096, 8
ST = 32
ET = 8
FB = 32
SCH = 512
NCH = S // SCH
NSB = SCH // 128
WPADW = 4352
LNEPS = 1e-5
EXPERT_ORDER = [7, 6, 5, 4, 3, 2, 1, 0]


class Arena:

    def __init__(self, nc, ctx, nbytes):
        self.nc = nc
        self.nbytes = nbytes
        self.h = ctx.enter_context(nc.sbuf_tensor("arena", [128, nbytes], U8))
        self.base = nc.lookup_mloc(self.h).addr

    def carve(self, name, off, shape, dtype):
        per_part = int(np.prod(shape[1:])) * mybir.dt.size(dtype)
        assert off + per_part <= self.nbytes, (name, off, per_part)
        return self.nc.alloc_sbuf_tensor_at(name, shape, dtype, offset=self.base + off)


class Em:
    ENGINES = ("tensor", "vector", "scalar", "sync", "gpsimd")

    def __init__(self, B):
        self.B = B
        self.lists = {k: [] for k in self.ENGINES}

    def op(self, eng, fn, inc=None):
        semh = kk = None
        if inc is not None:
            sem, kk = inc
            self.B.c[sem] += kk
            semh = self.B.sems[sem]

        def run(e, fn=fn, semh=semh, kk=kk):
            instr = fn(e)
            if semh is not None:
                instr.then_inc(semh, kk)
        self.lists[eng].append(run)
        return self.B.c[inc[0]] if inc is not None else None

    def wait(self, eng, sem, val):
        if val is None or val <= 0:
            return
        semh = self.B.sems[sem]
        assert val <= self.B.c[sem], ("wait beyond emitted", sem, val, self.B.c[sem])
        self.lists[eng].append(lambda e, s=semh, v=val: e.wait_ge(s, v))

    def flush(self, blk):
        for eng in self.ENGINES:
            lst = self.lists[eng]
            if not lst:
                continue
            getattr(blk, eng)(lambda e, lst=lst: [f(e) for f in lst])
        self.lists = {k: [] for k in self.ENGINES}


class Builder:
    def __init__(self, debug=(), phase_limit=99, conv_heads=H, experts=None):
        self.debug = set(debug)
        self.phase_limit = phase_limit
        self.conv_heads = conv_heads
        self.experts = list(experts) if experts is not None else list(EXPERT_ORDER)
        self.DMA_SEMS = ['dbc', 'dw1i', 'dw2i', 'dwg', 'dx0', 'dx1', 'dso0', 'dso1', 'dwp', 'dz0', 'dz1', 'dfl', 'dl0', 'dl1', 'dw1', 'dwr0', 'dwr1', 'dwr2', 'dwr3', 'dwr4', 'dwr5', 'dwr6', 'dwr7', 'dx2t', 'dy', 'ddbg', 'dpc0', 'dpc1', 'dps0', 'dps1', 'dx2', 'dso2']
        self.c = {k: 0 for k in (tuple(self.DMA_SEMS) + ("cT", "cV", "cS", "cG"))}
        self.sems = {}

    def declare_dram(self, nc):
        d = {}
        d["x"] = nc.dram_tensor("x", [S, E], F32, kind="ExternalInput").ap()
        d["wa"] = nc.dram_tensor("wa", [H, E], F32, kind="ExternalInput").ap()
        d["wv"] = nc.dram_tensor("wv", [E, E], F32, kind="ExternalInput").ap()
        d["wo"] = nc.dram_tensor("wo", [E, E], F32, kind="ExternalInput").ap()
        d["bo"] = nc.dram_tensor("bo", [1, E], F32, kind="ExternalInput").ap()
        d["w1"] = nc.dram_tensor("w1", [NE, E, FF], F32, kind="ExternalInput").ap()
        d["b1"] = nc.dram_tensor("b1", [NE, FF], F32, kind="ExternalInput").ap()
        d["w2"] = nc.dram_tensor("w2", [NE, FF, E], F32, kind="ExternalInput").ap()
        d["b2"] = nc.dram_tensor("b2", [NE, E], F32, kind="ExternalInput").ap()
        d["gamma"] = nc.dram_tensor("gamma", [1, E], F32, kind="ExternalInput").ap()
        d["beta"] = nc.dram_tensor("beta", [1, E], F32, kind="ExternalInput").ap()
        d["y"] = nc.dram_tensor("y", [S, E], F32, kind="ExternalOutput").ap()
        d["wpad_d"] = nc.dram_tensor("wpad_d", [H, WPADW], BF16, kind="Internal").ap()
        d["x2_d"] = nc.dram_tensor("x2_d", [S, E], F32, kind="Internal").ap()
        d["ln2t_d"] = nc.dram_tensor("ln2t_d", [E, S], BF16, kind="Internal").ap()
        d["w1b_d"] = nc.dram_tensor("w1b_d", [NE, E, FF], BF16, kind="Internal").ap()
        d["w2b_d"] = nc.dram_tensor("w2b_d", [NE, FF, E], BF16, kind="Internal").ap()
        dbg_shapes = {"ln1tf": ([128, 8 * 4096], BF16), "zsm": ([H, WPADW], BF16),
                      "vhm": ([128, 16 * 2048], BF16), "outnat": ([128, 32 * 1024], BF16),
                      "outt": ([128, 8 * 4096], BF16), "ht": ([128, FB * SCH], BF16),
                      "x2": ([S, E], F32), "ln2t": ([E, S], BF16),
                      "zt": ([16, 4096], F32), "expsb": ([16, 4096], F32),
                      "smax": ([16, 4], F32), "wat": ([128, ET * 16], BF16),
                      "zst": ([128, ST * 16], F32)}
        extra = {"zt": ["expsb", "smax", "wat", "zst"]}
        names = set(self.debug)
        for n in self.debug:
            names.update(extra.get(n, []))
        for name in sorted(names):
            shape, dt_ = dbg_shapes[name]
            d["dbg_" + name] = nc.dram_tensor("dbg_" + name, shape, dt_, kind="ExternalOutput").ap()
        return d

    def bcast_row(self, em, eng, dsem, dst, src_row_ap, wait=None):
        n = dst.shape[1]
        src = dataclasses.replace(src_row_ap, ap=[[0, 128], [1, n]])
        if wait is not None:
            em.wait(eng, wait[0], wait[1])
        return em.op(eng, lambda e, d=dst, s=src: e.dma_start(out=d[:, :], in_=s),
                     inc=(dsem, 16))

    def sfence(self, em, read_ap):
        em.wait("scalar", "cS", self.c["cS"])
        tok = self.fencetok
        self._fttog ^= 1
        col = self._fttog
        p = read_ap.shape[0]
        return em.op("scalar", lambda e, r=read_ap, p=p, col=col: e.activation(
            tok[0:p, col:col + 1], r, AF.Identity), inc=("cS", 1))

    def ln_chain_s(self, em, stats, col, w=1, eps=LNEPS):
        cc = lambda t: t[:, col:col + w]
        mean, rstd, mrstd = stats["mean"], stats["rstd"], stats["mrstd"]
        tA, tB = stats["tA"], stats["tB"]
        v1 = em.op("vector", lambda e: e.tensor_scalar_mul(cc(mean), cc(stats["sx"]), 1.0 / E), inc=("cV", 1))
        v2 = em.op("vector", lambda e: e.tensor_scalar(cc(tA), cc(stats["sq"]), 1.0 / E, float(eps),
                                                       op0=ALU.mult, op1=ALU.add), inc=("cV", 1))
        em.wait("vector", "cV", v1)
        v3 = em.op("vector", lambda e: e.tensor_mul(cc(tB), cc(mean), cc(mean)), inc=("cV", 1))
        em.wait("vector", "cV", v3)
        v4 = em.op("vector", lambda e: e.tensor_sub(cc(tA), cc(tA), cc(tB)), inc=("cV", 1))
        em.wait("scalar", "cV", v4)
        s1 = em.op("scalar", lambda e: e.activation(cc(tB), cc(tA), AF.Ln), inc=("cS", 1))
        em.wait("scalar", "cS", s1)
        s2 = em.op("scalar", lambda e: e.activation(cc(rstd), cc(tB), AF.Exp, scale=-0.5), inc=("cS", 1))
        f = self.sfence(em, cc(rstd)[:, 0:1])
        em.wait("vector", "cS", f)
        v5a = em.op("vector", lambda e: e.tensor_mul(cc(mrstd), cc(mean), cc(rstd)), inc=("cV", 1))
        em.wait("vector", "cV", v5a)
        v5 = em.op("vector", lambda e: e.tensor_scalar_mul(cc(mrstd), cc(mrstd), -1.0), inc=("cV", 1))
        return v5

    def ln_chain(self, em, stats, col, w=1, eps=LNEPS):
        cc = lambda t: t[:, col:col + w]
        mean, rstd, mrstd = stats["mean"], stats["rstd"], stats["mrstd"]
        tA, tB = stats["tA"], stats["tB"]
        I32 = mybir.dt.int32
        last = None

        def vop(fn):
            nonlocal last
            if last is not None:
                em.wait("vector", "cV", last)
            last = em.op("vector", fn, inc=("cV", 1))
            return last
        vop(lambda e: e.tensor_scalar_mul(cc(mean), cc(stats["sx"]), 1.0 / E))
        vop(lambda e: e.tensor_scalar(cc(tA), cc(stats["sq"]), 1.0 / E, float(eps),
                                      op0=ALU.mult, op1=ALU.add))
        vop(lambda e: e.tensor_mul(cc(tB), cc(mean), cc(mean)))
        vop(lambda e: e.tensor_sub(cc(tA), cc(tA), cc(tB)))
        vop(lambda e: e.tensor_single_scalar(cc(rstd).bitcast(I32), cc(tA).bitcast(I32),
                                             1, ALU.arith_shift_right))
        vop(lambda e: e.tensor_scalar(cc(rstd).bitcast(I32), cc(rstd).bitcast(I32),
                                      -1, 0x5F3759DF, op0=ALU.mult, op1=ALU.add))
        for _ in range(2):
            vop(lambda e: e.tensor_mul(cc(tB), cc(tA), cc(rstd)))
            vop(lambda e: e.tensor_mul(cc(tB), cc(tB), cc(rstd)))
            vop(lambda e: e.tensor_scalar(cc(tB), cc(tB), -0.5, 1.5, op0=ALU.mult, op1=ALU.add))
            vop(lambda e: e.tensor_mul(cc(rstd), cc(rstd), cc(tB)))
        vop(lambda e: e.tensor_mul(cc(mrstd), cc(mean), cc(rstd)))
        v5 = vop(lambda e: e.tensor_scalar_mul(cc(mrstd), cc(mrstd), -1.0))
        return v5

    def build(self):
        nc = bass.Bass("TRN2", target_bir_lowering=False)
        dram = self.declare_dram(nc)
        self.nc = nc
        with ExitStack() as ctx:
            self._build_inner(nc, dram, ctx)
        return nc

    def _build_inner(self, nc, dram, ctx):
        EC = ctx.enter_context
        for k in self.c:
            self.sems[k] = EC(nc.semaphore(k))

        ar = Arena(nc, ctx, 151552)
        S0, S1, S2 = 0, 65536, 86016

        wot = EC(nc.sbuf_tensor("wot", [128, 8 * 1024], BF16))
        junk2 = EC(nc.sbuf_tensor("junk", [128, 2 * 1024], BF16))
        self._jktog = 0
        def junk_ap():
            self._jktog ^= 1
            return junk2[:, self._jktog * 1024:(self._jktog + 1) * 1024]
        zst = EC(nc.sbuf_tensor("zst", [128, ST * 16], F32))
        b1T = EC(nc.sbuf_tensor("b1T", [128, FB * NE], F32))
        wat = EC(nc.sbuf_tensor("wat", [128, ET * 16], BF16))
        ident_bf = EC(nc.sbuf_tensor("ident_bf", [128, 128], BF16))
        jmat_bf = EC(nc.sbuf_tensor("jmat_bf", [128, 128], BF16))
        jmat_f = EC(nc.sbuf_tensor("jmat_f", [128, 128], F32))
        ident_f = EC(nc.sbuf_tensor("ident_f", [128, 128], F32))
        gamma_rep = EC(nc.sbuf_tensor("gamma_rep", [128, 1024], F32))
        beta_rep = EC(nc.sbuf_tensor("beta_rep", [128, 1024], F32))
        b2rep = EC(nc.sbuf_tensor("b2rep", [128, 1024], F32))
        self.fencetok = EC(nc.sbuf_tensor("fencetok", [128, 2], F32))
        self._fttog = 0

        def stat_block(prefix):
            return {k: EC(nc.sbuf_tensor(prefix + k, [128, ST], F32))
                    for k in ("sx", "sq", "mean", "rstd", "mrstd", "tA", "tB")}
        st1 = stat_block("s1_")
        st4 = stat_block("s4_")
        st2 = stat_block("s2_")
        stI = stat_block("sI_")
        stO = stat_block("sO_")
        smax = EC(nc.sbuf_tensor("smax", [16, 4], F32))

        attn_psum = ExitStack()
        pv0 = attn_psum.enter_context(nc.psum_tensor("pv0", [128, 1024], F32))
        pv1 = attn_psum.enter_context(nc.psum_tensor("pv1", [128, 1024], F32))
        pc2 = attn_psum.enter_context(nc.psum_tensor("pc2", [128, 1024], F32))
        psT = attn_psum.enter_context(nc.psum_tensor("psT", [128, 2048], BF16))
        pvs = [pv0, pv1]

        blk = EC(nc.Block())
        em = Em(self)

        def mk_ident(t, anti):
            m = em.op("gpsimd", lambda e, t=t: e.memset(t[:, :], 0.0), inc=("cG", 1))
            em.wait("gpsimd", "cG", m)
            return em.op("gpsimd", lambda e, t=t, anti=anti: e.affine_select(
                out=t[:, :], in_=t[:, :], compare_op=ALU.not_equal,
                fill=1.0, base=(-127 if anti else 0),
                pattern=[[1 if anti else -1, 128]], channel_multiplier=1), inc=("cG", 1))
        gid_ibf = mk_ident(ident_bf, False)
        gid_jbf = mk_ident(jmat_bf, True)
        em.wait("vector", "cG", gid_ibf)
        vc_if = em.op("vector", lambda e: e.tensor_copy(ident_f[:, :], ident_bf[:, :]), inc=("cV", 1))
        em.wait("vector", "cG", gid_jbf)
        vc_jf = em.op("vector", lambda e: e.tensor_copy(jmat_f[:, :], jmat_bf[:, :]), inc=("cV", 1))
        gid = gid_jbf

        self.bcast_row(em, "sync", "dbc", gamma_rep, dram["gamma"])
        self.bcast_row(em, "sync", "dbc", beta_rep, dram["beta"])
        self.bcast_row(em, "sync", "dbc", b2rep, dram["b2"][0:1, :])

        b1_sb = ar.carve("b1_sb", S0, [8, FF], F32)
        wa_sb = ar.carve("wa_sb", S0 + 16384, [16, E], F32)
        wa_bf = ar.carve("wa_bf", S0 + 20480, [16, E], BF16)
        wnat = ar.carve("wnat", S0 + 24576, [128, 8 * 1024], BF16)
        wvt = ar.carve("wvt", S1, [128, 8 * 1024], BF16)

        d1 = em.op("sync", lambda e: e.dma_start(out=b1_sb[:, :], in_=dram["b1"]), inc=("dw1i", 16))
        d2 = em.op("sync", lambda e: e.dma_start(out=wa_sb[:, :], in_=dram["wa"]), inc=("dw2i", 16))
        wv_src = dram["wv"].rearrange("(t p) e -> p t e", p=128)
        d3 = em.op("gpsimd", lambda e: e.dma_start(
            out=wnat[:, :].rearrange("p (t e) -> p t e", t=8), in_=wv_src), inc=("dwg", 16))

        def emit_wT(dst, src_dG, prev_psT_reader=None):
            evs = {}
            cnt = 0
            em.wait("tensor", "dwg", src_dG)
            em.wait("tensor", "cG", gid_ibf)
            if prev_psT_reader is not None:
                em.wait("tensor", "cV", prev_psT_reader)
            for t_ in range(8):
                for q in range(ET):
                    half = cnt % 2
                    if cnt >= 2:
                        em.wait("tensor", "cV", evs[cnt - 2])
                    tt = em.op("tensor", lambda e, t_=t_, q=q, half=half: e.transpose(
                        psT[:, half * 1024:half * 1024 + 128],
                        wnat[:, t_ * 1024 + q * 128: t_ * 1024 + q * 128 + 128],
                        ident_bf[:, :]), inc=("cT", 1))
                    em.wait("vector", "cT", tt)
                    evs[cnt] = em.op("vector", lambda e, t_=t_, q=q, half=half, dst=dst: e.tensor_copy(
                        dst[:, q * 1024 + t_ * 128: q * 1024 + t_ * 128 + 128],
                        psT[:, half * 1024:half * 1024 + 128]), inc=("cV", 1))
                    cnt += 1
            return evs[cnt - 1]

        wv_done = emit_wT(wvt, d3)
        em.wait("gpsimd", "cV", wv_done)
        wo_src = dram["wo"].rearrange("(t p) e -> p t e", p=128)
        d4 = em.op("gpsimd", lambda e: e.dma_start(
            out=wnat[:, :].rearrange("p (t e) -> p t e", t=8), in_=wo_src), inc=("dwg", 16))
        init_done_v = emit_wT(wot, d4)
        em.wait("vector", "dw2i", d2)
        wacast = em.op("vector", lambda e: e.tensor_copy(wa_bf[:, :], wa_sb[:, :]), inc=("cV", 1))
        em.wait("tensor", "cV", wacast)
        em.wait("tensor", "cV", init_done_v)
        wa_ev = {}
        for q in range(ET):
            half = q % 2
            if q >= 2:
                em.wait("tensor", "cV", wa_ev[q - 2])
            t = em.op("tensor", lambda e, q=q, half=half: e.transpose(
                psT[:, half * 1024:half * 1024 + 16],
                wa_bf[0:16, q * 128:q * 128 + 128], ident_bf[0:16, 0:16]), inc=("cT", 1))
            em.wait("vector", "cT", t)
            wa_ev[q] = em.op("vector", lambda e, q=q, half=half: e.tensor_copy(
                wat[:, q * 16:q * 16 + 16], psT[:, half * 1024:half * 1024 + 16]), inc=("cV", 1))

        em.wait("tensor", "dw1i", d1)
        em.wait("tensor", "cV", vc_if)
        b1ev = {}
        for fb in range(FB):
            pv = pvs[fb % 2]
            if fb >= 2:
                em.wait("tensor", "cV", b1ev[fb - 2])
            t = em.op("tensor", lambda e, fb=fb, pv=pv: e.transpose(
                pv[:, 0:8], b1_sb[0:8, fb * 128:fb * 128 + 128], ident_f[0:8, 0:8]),
                inc=("cT", 1))
            em.wait("vector", "cT", t)
            b1ev[fb] = em.op("vector", lambda e, fb=fb, pv=pv: e.tensor_copy(
                b1T[:, fb * 8:fb * 8 + 8], pv[:, 0:8]), inc=("cV", 1))

        em.flush(blk)

        if self.phase_limit < 1:
            self._finish(blk, em)
            return

        xb = ar.carve("xb", S2, [128, 2 * 1024], F32)
        ln1bf = ar.carve("ln1bf", S2 + 8192, [128, 2 * 1024], BF16)
        ln1tf = ar.carve("ln1tf", S0, [128, 8 * 4096], BF16)

        red_done, sq_done = {}, {}
        for stt in range(ST):
            buf = stt % 2
            if stt >= 2:
                em.wait("sync", "cV", red_done[stt - 2])
                em.wait("sync", "cS", sq_done[stt - 2])
            ld = em.op("sync", lambda e, stt=stt, buf=buf: e.dma_start(
                out=xb[:, buf * 1024:buf * 1024 + 1024],
                in_=dram["x"][stt * 128:stt * 128 + 128, :]), inc=("dx%d" % buf, 16))
            em.wait("vector", "dx%d" % buf, ld)
            red_done[stt] = em.op("vector", lambda e, stt=stt, buf=buf: e.reduce_sum(
                st1["sx"][:, stt:stt + 1], xb[:, buf * 1024:buf * 1024 + 1024], axis=AX.X), inc=("cV", 1))
            em.wait("scalar", "dx%d" % buf, ld)
            sq_done[stt] = em.op("scalar", lambda e, stt=stt, buf=buf: e.activation(
                junk_ap(), xb[:, buf * 1024:buf * 1024 + 1024], AF.Square,
                accum_out=st1["sq"][:, stt:stt + 1]), inc=("cS", 1))
        fsq = self.sfence(em, st1["sq"][:, ST - 1:ST])
        em.wait("vector", "cV", red_done[ST - 1])
        em.wait("vector", "cS", fsq)
        v1 = em.op("vector", lambda e: e.tensor_scalar_mul(st1["mean"][:, :], st1["sx"][:, :], 1.0 / E), inc=("cV", 1))
        v2 = em.op("vector", lambda e: e.tensor_scalar(st1["tA"][:, :], st1["sq"][:, :], 1.0 / E, float(LNEPS),
                                                       op0=ALU.mult, op1=ALU.add), inc=("cV", 1))
        em.wait("vector", "cV", v2)
        v3 = em.op("vector", lambda e: e.tensor_mul(st1["tB"][:, :], st1["mean"][:, :], st1["mean"][:, :]), inc=("cV", 1))
        em.wait("vector", "cV", v3)
        v4 = em.op("vector", lambda e: e.tensor_sub(st1["tA"][:, :], st1["tA"][:, :], st1["tB"][:, :]), inc=("cV", 1))
        em.wait("scalar", "cV", v4)
        s1_ = em.op("scalar", lambda e: e.activation(st1["tB"][:, :], st1["tA"][:, :], AF.Ln), inc=("cS", 1))
        em.wait("scalar", "cS", s1_)
        s2_ = em.op("scalar", lambda e: e.activation(st1["rstd"][:, :], st1["tB"][:, :], AF.Exp, scale=-0.5), inc=("cS", 1))
        f1_ = self.sfence(em, st1["rstd"][:, ST - 1:ST])
        em.wait("vector", "cS", f1_)
        ln1tiny0 = em.op("vector", lambda e: e.tensor_mul(st1["mrstd"][:, :], st1["mean"][:, :], st1["rstd"][:, :]), inc=("cV", 1))
        em.wait("vector", "cV", ln1tiny0)
        ln1tiny = em.op("vector", lambda e: e.tensor_scalar_mul(st1["mrstd"][:, :], st1["mrstd"][:, :], -1.0), inc=("cV", 1))

        ap_done, tr1_done, ev1c = {}, {}, {}
        for stt in range(ST):
            buf = stt % 2
            half = stt % 2
            if stt >= 2:
                em.wait("sync", "cV", ap_done[stt - 2])
            else:
                em.wait("sync", "cV", red_done[ST - 2 + stt])
                em.wait("sync", "cS", sq_done[ST - 2 + stt])
            ld = em.op("sync", lambda e, stt=stt, buf=buf: e.dma_start(
                out=xb[:, buf * 1024:buf * 1024 + 1024],
                in_=dram["x"][stt * 128:stt * 128 + 128, :]), inc=("dx%d" % buf, 16))
            em.wait("vector", "dx%d" % buf, ld)
            em.wait("vector", "cV", ln1tiny)
            if stt >= 2:
                em.wait("vector", "cT", tr1_done[stt - 2])
            ap_done[stt] = em.op("vector", lambda e, stt=stt, buf=buf: e.tensor_scalar(
                ln1bf[:, buf * 1024:buf * 1024 + 1024], xb[:, buf * 1024:buf * 1024 + 1024],
                st1["rstd"][:, stt:stt + 1], st1["mrstd"][:, stt:stt + 1],
                op0=ALU.mult, op1=ALU.add), inc=("cV", 1))
            em.wait("tensor", "cV", ap_done[stt])
            if stt == 0:
                em.wait("tensor", "cG", gid_jbf)
            if stt >= 2:
                em.wait("tensor", "cV", ev1c[stt - 2])
            t = None
            for q in range(ET):
                t = em.op("tensor", lambda e, q=q, buf=buf, half=half: e.transpose(
                    psT[:, half * 1024 + q * 128: half * 1024 + q * 128 + 128],
                    ln1bf[:, buf * 1024 + q * 128: buf * 1024 + q * 128 + 128],
                    jmat_bf[:, :]), inc=("cT", 1))
            tr1_done[stt] = t
            em.wait("vector", "cT", t)
            ev1c[stt] = em.op("vector", lambda e, stt=stt, half=half: e.tensor_copy(
                ln1tf[:, :].rearrange("p (q s) -> p q s", q=ET)[:, :, stt * 128:stt * 128 + 128],
                psT[:, half * 1024:half * 1024 + 1024].rearrange("p (q m) -> p q m", q=ET)), inc=("cV", 1))
        em.flush(blk)

        if "ln1tf" in self.debug:
            em.wait("sync", "cV", ev1c[ST - 1])
            em.op("sync", lambda e: e.dma_start(out=dram["dbg_ln1tf"], in_=ln1tf[:, :]), inc=("ddbg", 16))
            em.flush(blk)
        if self.phase_limit < 2:
            self._finish(blk, em)
            return

        vhm = ar.carve("vhm", S2, [128, H * 2 * 1024], BF16)
        vev, zev = {}, {}
        for stt in range(ST):
            pv = pvs[stt % 2]
            zoff = (stt % 2) * 512
            em.wait("tensor", "cV", ev1c[stt])
            em.wait("tensor", "cV", init_done_v)
            if stt >= 2:
                em.wait("tensor", "cV", vev[stt - 2])
                em.wait("tensor", "cV", zev[stt - 2])
            for q in range(ET):
                lhs = ln1tf[:, q * 4096 + stt * 128: q * 4096 + stt * 128 + 128]
                em.op("tensor", lambda e, q=q, pv=pv, lhs=lhs: e.matmul(
                    pv[:, 0:512], lhs, wvt[:, q * 1024:q * 1024 + 512],
                    start=(q == 0), stop=(q == ET - 1), skip_group_check=True))
                em.op("tensor", lambda e, q=q, pv=pv, lhs=lhs: e.matmul(
                    pv[:, 512:1024], lhs, wvt[:, q * 1024 + 512:q * 1024 + 1024],
                    start=(q == 0), stop=(q == ET - 1), skip_group_check=True))
                em.op("tensor", lambda e, q=q, zoff=zoff, lhs=lhs: e.matmul(
                    pc2[:, zoff:zoff + 16], lhs, wat[:, q * 16:q * 16 + 16],
                    start=(q == 0), stop=(q == ET - 1), skip_group_check=True),
                    inc=("cT", 1) if q == ET - 1 else None)
            grp = self.c["cT"]
            em.wait("vector", "cT", grp)
            vev[stt] = em.op("vector", lambda e, stt=stt, pv=pv: e.tensor_copy(
                vhm[:, :].rearrange("p (h c d) -> p h c d", h=H, d=D)[:, :, stt, :],
                pv[:, :].rearrange("p (h d) -> p h d", h=H)), inc=("cV", 1))
            zev[stt] = em.op("vector", lambda e, stt=stt, zoff=zoff: e.tensor_copy(
                zst[:, stt * 16:stt * 16 + 16], pc2[:, zoff:zoff + 16]), inc=("cV", 1))
        ph2_all_cT = self.c["cT"]
        em.flush(blk)

        zT = ar.carve("zT", S0, [16, 4096], F32)
        exp_sb = ar.carve("exp_sb", S0 + 16384, [16, 4096], F32)
        wpad_sb = ar.carve("wpad_sb", S0 + 32768, [16, WPADW], BF16)

        em.wait("gpsimd", "cT", ph2_all_cT)
        wp0 = em.op("gpsimd", lambda e: e.memset(wpad_sb[:, :], 0.0), inc=("cG", 1))
        ztev = {}
        em.wait("tensor", "cV", zev[ST - 1])
        em.wait("tensor", "cV", vc_jf)
        for stt in range(ST):
            pv = pvs[stt % 2]
            if stt >= 2:
                em.wait("tensor", "cV", ztev[stt - 2])
            t = em.op("tensor", lambda e, stt=stt, pv=pv: e.transpose(
                pv[0:16, 0:128], zst[:, stt * 16:stt * 16 + 16], jmat_f[:, :]), inc=("cT", 1))
            em.wait("vector", "cT", t)
            ztev[stt] = em.op("vector", lambda e, stt=stt, pv=pv: e.tensor_copy(
                zT[0:16, stt * 128:stt * 128 + 128], pv[0:16, 0:128]), inc=("cV", 1))
        em.wait("vector", "cV", ztev[ST - 1])
        r1 = em.op("vector", lambda e: e.reduce_max(smax[0:16, 0:1], zT[0:16, :], axis=AX.X, negate=True), inc=("cV", 1))
        em.wait("vector", "cV", r1)
        r2 = em.op("vector", lambda e: e.tensor_scalar_mul(smax[0:16, 1:2], smax[0:16, 0:1], 0.125), inc=("cV", 1))
        em.wait("scalar", "cV", r2)
        r3 = em.op("scalar", lambda e: e.activation(
            exp_sb[0:16, :], zT[0:16, :], AF.Exp, scale=0.125, bias=smax[0:16, 1:2]), inc=("cS", 1))
        em.wait("scalar", "cS", r3)
        r3f = em.op("scalar", lambda e: e.activation(
            smax[0:16, 2:3], exp_sb[0:16, 4095:4096], AF.Identity), inc=("cS", 1))
        em.wait("vector", "cS", r3f)
        r3b = em.op("vector", lambda e: e.reduce_sum(smax[0:16, 2:3], exp_sb[0:16, :], axis=AX.X), inc=("cV", 1))
        em.wait("vector", "cV", r3b)
        r4 = em.op("vector", lambda e: e.reciprocal(smax[0:16, 3:4], smax[0:16, 2:3]), inc=("cV", 1))
        em.wait("vector", "cV", r4)
        em.wait("vector", "cG", wp0)
        r5 = em.op("vector", lambda e: e.tensor_scalar(
            wpad_sb[0:16, 127:127 + 4096], exp_sb[0:16, :], smax[0:16, 3:4], None,
            op0=ALU.mult), inc=("cV", 1))
        em.wait("sync", "cV", r5)
        wpD = em.op("sync", lambda e: e.dma_start(out=dram["wpad_d"], in_=wpad_sb[:, :]), inc=("dwp", 16))
        em.flush(blk)

        if "zsm" in self.debug:
            em.wait("sync", "dwp", wpD)
            em.op("sync", lambda e: e.dma_start(out=dram["dbg_zsm"], in_=wpad_sb[:, :]), inc=("ddbg", 16))
            em.flush(blk)
        if "zt" in self.debug:
            em.wait("sync", "dwp", wpD)
            em.op("sync", lambda e: e.dma_start(out=dram["dbg_zt"], in_=zT[0:16, :]), inc=("ddbg", 16))
            em.op("sync", lambda e: e.dma_start(out=dram["dbg_expsb"], in_=exp_sb[0:16, :]), inc=("ddbg", 16))
            em.op("sync", lambda e: e.dma_start(out=dram["dbg_smax"], in_=smax[0:16, :]), inc=("ddbg", 16))
            em.op("sync", lambda e: e.dma_start(out=dram["dbg_wat"], in_=wat[:, :]), inc=("ddbg", 16))
            em.op("sync", lambda e: e.dma_start(out=dram["dbg_zst"], in_=zst[:, :]), inc=("ddbg", 16))
            em.flush(blk)
        if self.phase_limit < 3:
            self._finish(blk, em)
            return

        Zbuf = ar.carve("Zbuf", S1, [128, 2 * 4096], BF16)
        out_nat = ar.carve("out_nat", S0, [128, ST * 1024], BF16)
        rot = [pv0, pv1, pc2]
        zld, zconsumed, cv_evac = {}, {}, {}

        def emit_zbuild(h):
            buf = h % 2
            em.wait("scalar", "dwp", wpD)
            if h >= 2:
                em.wait("scalar", "cT", zconsumed[h - 2])
            src = dataclasses.replace(dram["wpad_d"], offset=h * WPADW, ap=[[1, 128], [1, 4096]])
            zld[h] = em.op("scalar", lambda e, buf=buf, src=src: e.dma_start(
                out=Zbuf[:, buf * 4096:buf * 4096 + 4096], in_=src), inc=("dz%d" % buf, 16))

        NHC = self.conv_heads
        emit_zbuild(0)
        if NHC > 1:
            emit_zbuild(1)
        rix = 0
        for h in range(NHC):
            buf = h % 2
            em.wait("tensor", "dz%d" % (h % 2), zld[h])
            em.wait("tensor", "cV", ztev[ST - 1])
            em.wait("tensor", "cV", vev[ST - 1])
            for half in range(2):
                ps = rot[rix % 3]
                if rix >= 3:
                    em.wait("tensor", "cV", cv_evac[rix - 3])
                base = 1024 * half
                ks = range(16) if half == 0 else range(32)
                grp = None
                for k in ks:
                    lo = max(64 * k, base)
                    hi = base + 1024
                    cur = lo
                    while cur < hi:
                        nxt = min(hi, (cur // 512 + 1) * 512)
                        a, b = cur, nxt
                        cur = nxt
                        bank = a // 512
                        klast = 8 * bank + 7
                        stop = (k == min(klast, ks[-1]))
                        is_final = (k == ks[-1] and b == hi)
                        grp = em.op("tensor", lambda e, a=a, b=b, k=k, ps=ps, base=base, h=h, buf=buf, stop=stop: e.matmul(
                            ps[:, a - base:b - base],
                            Zbuf[:, buf * 4096 + k * 128: buf * 4096 + k * 128 + 128],
                            vhm[:, h * 2048 + a - 64 * k: h * 2048 + b - 64 * k],
                            start=(k == 0), stop=stop, skip_group_check=True),
                            inc=("cT", 1) if is_final else None)
                if half == 1:
                    zconsumed[h] = grp
                    if h + 2 < NHC:
                        emit_zbuild(h + 2)
                em.wait("vector", "cT", grp)
                cv_evac[rix] = em.op("vector", lambda e, h=h, half=half, ps=ps: e.tensor_copy(
                    out_nat[:, :].rearrange("p (c f) -> p c f", f=1024)[:, 16 * half:16 * half + 16, h * 64:h * 64 + 64],
                    ps[:, :].rearrange("p (c d) -> p c d", d=D)), inc=("cV", 1))
                rix += 1
        conv_done = self.c["cV"]
        em.flush(blk)

        if "outnat" in self.debug:
            em.wait("sync", "cV", conv_done)
            em.op("sync", lambda e: e.dma_start(out=dram["dbg_outnat"], in_=out_nat[:, :]), inc=("ddbg", 16))
            em.flush(blk)
        if "vhm" in self.debug:
            em.wait("sync", "cV", conv_done)
            em.op("sync", lambda e: e.dma_start(out=dram["dbg_vhm"], in_=vhm[:, :]), inc=("ddbg", 16))
            em.flush(blk)
        if self.phase_limit < 4:
            self._finish(blk, em)
            return

        outT = ar.carve("outT", S2, [128, 8 * 4096], BF16)
        ev3c = {}
        for stt in range(ST):
            half = stt % 2
            em.wait("tensor", "cV", conv_done)
            if stt >= 2:
                em.wait("tensor", "cV", ev3c[stt - 2])
            for q in range(ET):
                em.op("tensor", lambda e, stt=stt, q=q, half=half: e.transpose(
                    psT[:, half * 1024 + q * 128: half * 1024 + q * 128 + 128],
                    out_nat[:, stt * 1024 + q * 128: stt * 1024 + q * 128 + 128],
                    ident_bf[:, :]), inc=("cT", 1) if q == ET - 1 else None)
            em.wait("vector", "cT", self.c["cT"])
            ev3c[stt] = em.op("vector", lambda e, stt=stt, half=half: e.tensor_copy(
                outT[:, :].rearrange("p (q s) -> p q s", q=ET)[:, :, stt * 128:stt * 128 + 128],
                psT[:, half * 1024:half * 1024 + 1024].rearrange("p (q m) -> p q m", q=ET)), inc=("cV", 1))
        em.flush(blk)
        if "outt" in self.debug:
            em.wait("sync", "cV", ev3c[ST - 1])
            em.op("sync", lambda e: e.dma_start(out=dram["dbg_outt"], in_=outT[:, :]), inc=("ddbg", 16))
            em.flush(blk)

        pstg_f = ar.carve("pstg_f", S1, [128, 2 * 1024], F32)
        pstg_b = ar.carve("pstg_b", S1 + 8192, [128, 2 * 1024], BF16)
        self._pc_i = 0
        self._pc_cast = {}
        self._pc_st = {}
        self._pc_dps_snap = {}

        def emit_precast_pieces(eidx, npieces):
            for _ in range(npieces):
                i = self._pc_i
                if i >= 64:
                    return
                self._pc_i = i + 1
                p = i % 2
                if i < 32:
                    dt, qq = i // 4, i % 4
                    src = dram["w1"][eidx, dt * 128:dt * 128 + 128, qq * 1024:qq * 1024 + 1024]
                    dst = dram["w1b_d"][eidx, dt * 128:dt * 128 + 128, qq * 1024:qq * 1024 + 1024]
                else:
                    rr = i - 32
                    src = dram["w2"][eidx, rr * 128:rr * 128 + 128, :]
                    dst = dram["w2b_d"][eidx, rr * 128:rr * 128 + 128, :]
                if i >= 2:
                    em.wait("sync", "cV", self._pc_cast[i - 2])
                ld = em.op("sync", lambda e, src=src, p=p: e.dma_start(
                    out=pstg_f[:, p * 1024:p * 1024 + 1024], in_=src), inc=("dpc%d" % p, 16))
                em.wait("vector", "dpc%d" % p, ld)
                if i >= 2:
                    em.wait("vector", "dps%d" % p, self._pc_st[i - 2])
                self._pc_cast[i] = em.op("vector", lambda e, p=p: e.tensor_copy(
                    pstg_b[:, p * 1024:p * 1024 + 1024], pstg_f[:, p * 1024:p * 1024 + 1024]),
                    inc=("cV", 1))
                em.wait("sync", "cV", self._pc_cast[i])
                self._pc_st[i] = em.op("sync", lambda e, dst=dst, p=p: e.dma_start(
                    out=dst, in_=pstg_b[:, p * 1024:p * 1024 + 1024]), inc=("dps%d" % p, 16))

        def finish_precast(eidx):
            emit_precast_pieces(eidx, 64)
            self._pc_i = 0
            self._pc_dps_snap[eidx] = (self.c["dps0"], self.c["dps1"])

        lnstg = ar.carve("lnstg", S0, [128, ET * 512], BF16)
        xb4 = ar.carve("xb4", S0 + 8192, [128, 3 * 1024], F32)
        x2b = ar.carve("x2b", S0 + 20480, [128, 3 * 1024], F32)
        osb = ar.carve("osb", S0 + 32768, [128, 3 * 1024], F32)
        alb = ar.carve("alb", S0 + 45056, [128, 2 * 1024], F32)
        ln2bf = ar.carve("ln2bf", S0 + 53248, [128, 3 * 1024], BF16)
        bo_rep = ar.carve("bo_rep", S0 + 59392, [128, 1024], F32)

        out_nat_free = ev3c[ST - 1]
        bo_ld = self.bcast_row(em, "sync", "dbc", bo_rep, dram["bo"], wait=("cV", out_nat_free))
        oev, x2st, x2done, lnf, rs4_, rs2_ = {}, {}, {}, {}, {}, {}
        al_of, lnap_of = {}, {}
        stgflush = None
        lnevac = None
        pvs3 = [pv0, pv1, pc2]
        for stt in range(ST):
            buf = stt % 3
            ab = stt % 2
            half = stt % 2
            pv = pvs3[stt % 3]
            if self.phase_limit >= 5 and stt % 2 == 1:
                emit_precast_pieces(self.experts[0], 4)
            if stt >= 3:
                em.wait("sync", "cV", x2done[stt - 3])
            xld = em.op("sync", lambda e, stt=stt, buf=buf: e.dma_start(
                out=xb4[:, buf * 1024:buf * 1024 + 1024],
                in_=dram["x"][stt * 128:stt * 128 + 128, :]), inc=("dx%d" % buf, 16))
            em.wait("tensor", "cV", ev3c[stt])
            if stt >= 3:
                em.wait("tensor", "cV", oev[stt - 3])
            for q in range(ET):
                lhs = outT[:, q * 4096 + stt * 128: q * 4096 + stt * 128 + 128]
                em.op("tensor", lambda e, pv=pv, lhs=lhs, q=q: e.matmul(
                    pv[:, 0:512], lhs, wot[:, q * 1024:q * 1024 + 512],
                    start=(q == 0), stop=(q == ET - 1), skip_group_check=True))
                em.op("tensor", lambda e, pv=pv, lhs=lhs, q=q: e.matmul(
                    pv[:, 512:1024], lhs, wot[:, q * 1024 + 512:q * 1024 + 1024],
                    start=(q == 0), stop=(q == ET - 1), skip_group_check=True),
                    inc=("cT", 1) if q == ET - 1 else None)
            em.wait("vector", "cT", self.c["cT"])
            em.wait("vector", "dbc", bo_ld)
            oev[stt] = em.op("vector", lambda e, stt=stt, buf=buf, pv=pv: e.scalar_tensor_tensor(
                osb[:, buf * 1024:buf * 1024 + 1024], bo_rep[:, :], 1.0, pv[:, :],
                op0=ALU.mult, op1=ALU.add,
                accum_out=st4["sx"][:, stt:stt + 1]), inc=("cV", 1))
            em.wait("vector", "cV", oev[stt])
            rs4_[stt] = em.op("vector", lambda e, stt=stt, buf=buf: e.scalar_tensor_tensor(
                junk_ap(), osb[:, buf * 1024:buf * 1024 + 1024], 1.0,
                osb[:, buf * 1024:buf * 1024 + 1024], op0=ALU.mult, op1=ALU.mult,
                accum_out=st4["sq"][:, stt:stt + 1]), inc=("cV", 1))
            em.wait("vector", "cV", rs4_[stt])
            tiny4 = self.ln_chain(em, st4, stt)
            em.wait("vector", "cV", tiny4)
            al = em.op("vector", lambda e, stt=stt, buf=buf, ab=ab: e.tensor_scalar(
                alb[:, ab * 1024:ab * 1024 + 1024], osb[:, buf * 1024:buf * 1024 + 1024],
                st4["rstd"][:, stt:stt + 1], st4["mrstd"][:, stt:stt + 1],
                op0=ALU.mult, op1=ALU.add), inc=("cV", 1))
            em.wait("vector", "dx%d" % buf, xld)
            em.wait("vector", "cV", al)
            if stt >= 3:
                em.wait("vector", "dso%d" % buf, x2st[stt - 3])
            x2c = em.op("vector", lambda e, stt=stt, buf=buf, ab=ab: e.scalar_tensor_tensor(
                x2b[:, buf * 1024:buf * 1024 + 1024], alb[:, ab * 1024:ab * 1024 + 1024], 1.0,
                xb4[:, buf * 1024:buf * 1024 + 1024],
                op0=ALU.mult, op1=ALU.add,
                accum_out=st2["sx"][:, stt:stt + 1]), inc=("cV", 1))
            x2done[stt] = x2c
            em.wait("sync", "cV", x2c)
            x2st[stt] = em.op("sync", lambda e, stt=stt, buf=buf: e.dma_start(
                out=dram["x2_d"][stt * 128:stt * 128 + 128, :],
                in_=x2b[:, buf * 1024:buf * 1024 + 1024]), inc=("dso%d" % buf, 16))
            em.wait("vector", "cV", x2c)
            rs2_[stt] = em.op("vector", lambda e, stt=stt, buf=buf: e.scalar_tensor_tensor(
                junk_ap(), x2b[:, buf * 1024:buf * 1024 + 1024], 1.0,
                x2b[:, buf * 1024:buf * 1024 + 1024], op0=ALU.mult, op1=ALU.mult,
                accum_out=st2["sq"][:, stt:stt + 1]), inc=("cV", 1))
            em.wait("vector", "cV", rs2_[stt])
            tiny2 = self.ln_chain(em, st2, stt)
            em.wait("vector", "cV", tiny2)
            if stt >= 3:
                em.wait("vector", "cT", lnf[stt - 3])
            lnap = em.op("vector", lambda e, stt=stt, buf=buf: e.tensor_scalar(
                ln2bf[:, buf * 1024:buf * 1024 + 1024], x2b[:, buf * 1024:buf * 1024 + 1024],
                st2["rstd"][:, stt:stt + 1], st2["mrstd"][:, stt:stt + 1],
                op0=ALU.mult, op1=ALU.add), inc=("cV", 1))
            em.wait("tensor", "cV", lnap)
            if stt >= 2:
                em.wait("tensor", "cV", lnevacs[stt - 2])
            t = None
            for q in range(ET):
                t = em.op("tensor", lambda e, q=q, buf=buf, half=half: e.transpose(
                    psT[:, half * 1024 + q * 128: half * 1024 + q * 128 + 128],
                    ln2bf[:, buf * 1024 + q * 128: buf * 1024 + q * 128 + 128],
                    ident_bf[:, :]), inc=("cT", 1) if q == ET - 1 else None)
            lnf[stt] = self.c["cT"]
            em.wait("vector", "cT", self.c["cT"])
            if stt % 4 == 0 and stt >= 4:
                em.wait("vector", "dfl", stgflush)
            if stt == 0:
                lnevacs = {}
            lnevacs[stt] = em.op("vector", lambda e, stt=stt, half=half: e.tensor_copy(
                lnstg[:, :].rearrange("p (q s) -> p q s", q=ET)[:, :, (stt % 4) * 128:(stt % 4) * 128 + 128],
                psT[:, half * 1024:half * 1024 + 1024].rearrange("p (q m) -> p q m", q=ET)), inc=("cV", 1))
            if stt % 4 == 3:
                grpi = stt // 4
                em.wait("scalar", "cV", lnevacs[stt])
                for qh in range(2):
                    dst = dataclasses.replace(
                        dram["ln2t_d"], offset=qh * 4 * 128 * S + grpi * 512,
                        ap=[[S, 128], [128 * S, 4], [1, 512]])
                    stgflush = em.op("scalar", lambda e, dst=dst, qh=qh: e.dma_start(
                        out=dst,
                        in_=lnstg[:, :].rearrange("p (q s) -> p q s", q=ET)[:, qh * 4:qh * 4 + 4, :]),
                        inc=("dfl", 16))
        if self.phase_limit >= 5:
            finish_precast(self.experts[0])
        ph4 = {k: self.c[k] for k in self.c}
        em.flush(blk)

        if "x2" in self.debug:
            em.wait("sync", "dso0", ph4["dso0"])
            em.wait("sync", "dso1", ph4["dso1"])
            em.op("sync", lambda e: e.dma_start(out=dram["dbg_x2"], in_=dram["x2_d"]), inc=("ddbg", 16))
            em.flush(blk)
        if "ln2t" in self.debug:
            em.wait("sync", "dfl", ph4["dfl"])
            em.op("sync", lambda e: e.dma_start(out=dram["dbg_ln2t"], in_=dram["ln2t_d"]), inc=("ddbg", 16))
            em.flush(blk)
        if self.phase_limit < 5:
            self._finish(blk, em)
            return

        attn_psum.close()
        p5 = ExitStack()
        psE = [p5.enter_context(nc.psum_tensor(f"psE{i}", [128, 1024], F32)) for i in range(4)]
        ctx.callback(p5.close)

        w1sb = ar.carve("w1sb", S0, [128, ET * 4096], BF16)
        hT = ar.carve("hT", S2, [128, FB * SCH], BF16)
        lnTb = ar.carve("lnTb", S2 + 32768, [128, 2 * ET * SCH], BF16)
        w2ring = ar.carve("w2ring", S2 + 49152, [128, 8 * 1024], BF16)
        eo_bf = ar.carve("eo_bf", S1, [128, 4 * 1024], BF16)
        x2t = ar.carve("x2t", S1 + 8192, [128, 1024], F32)
        tA = ar.carve("tA", S1 + 12288, [128, 1024], F32)
        tB = ar.carve("tB", S1 + 16384, [128, 1024], F32)

        for eng in ("tensor", "vector", "scalar", "sync", "gpsimd"):
            for s_ in (["cT", "cV", "cS"] + self.DMA_SEMS):
                em.wait(eng, s_, ph4[s_])

        def emit_w1_load(eidx):
            sn = self._pc_dps_snap[eidx]
            em.wait("sync", "dps0", sn[0])
            em.wait("sync", "dps1", sn[1])
            last = None
            for dt in range(ET):
                src = dram["w1b_d"][eidx, dt * 128:dt * 128 + 128, :]
                last = em.op("sync", lambda e, dt=dt, src=src: e.dma_start(
                    out=w1sb[:, dt * 4096:dt * 4096 + 4096], in_=src), inc=("dw1", 16))
            return last

        EORD = self.experts
        NEXP = len(EORD)
        w1_ready = emit_w1_load(EORD[0])

        def emit_lnT_load(c, slot, wait_cT):
            if wait_cT:
                em.wait("sync", "cT", wait_cT)
            last = None
            for qh in range(2):
                src = dataclasses.replace(
                    dram["ln2t_d"], offset=qh * 4 * 128 * S + c * SCH,
                    ap=[[S, 128], [128 * S, 4], [1, SCH]])
                last = em.op("sync", lambda e, src=src, slot=slot, qh=qh: e.dma_start(
                    out=lnTb[:, :].rearrange("p (b q s) -> p b q s", b=2, q=ET)[:, slot, qh * 4:qh * 4 + 4, :],
                    in_=src), inc=("dl%d" % slot, 16))
            return last

        ph1_group_cT = {}
        evac_cS = {}
        psE_free_cV = {i: 0 for i in range(4)}
        prev_x2_rd = None
        prev_yst = None
        lnT_slot_free = {0: None, 1: None}
        w2_free_cT = None
        lnT_ready = {0: emit_lnT_load(0, 0, None)}

        for ei, e_real in enumerate(EORD):
            is_last = (e_real == 0)
            for c in range(NCH):
                if ei + 1 < NEXP:
                    emit_precast_pieces(EORD[ei + 1], 8)
                    if c == NCH - 1:
                        finish_precast(EORD[ei + 1])
                nxt = c + 1
                if nxt < NCH:
                    lnT_ready[nxt] = emit_lnT_load(nxt, nxt % 2, lnT_slot_free[nxt % 2])
                elif ei + 1 < NEXP:
                    lnT_ready[0] = emit_lnT_load(0, 0, lnT_slot_free[0])

                em.wait("tensor", "dl%d" % (c % 2), lnT_ready[c])
                if c == 0:
                    em.wait("tensor", "dw1", w1_ready)
                slot = c % 2
                for fb in range(FB):
                    half = fb % 2
                    if fb >= 2:
                        em.wait("tensor", "cS", evac_cS[(ei, c, fb - 2)])
                    elif is_last and c > 0:
                        em.wait("tensor", "cV", psE_free_cV[0])
                    rhs = lnTb[:, :].rearrange("p (b q s) -> p b q s", b=2, q=ET)
                    for dt in range(ET):
                        em.op("tensor", lambda e, fb=fb, dt=dt, half=half, slot=slot, rhs=rhs: e.matmul(
                            psE[0][:, half * 512:half * 512 + 512],
                            w1sb[:, dt * 4096 + fb * 128: dt * 4096 + fb * 128 + 128],
                            rhs[:, slot, dt, :],
                            start=(dt == 0), stop=(dt == ET - 1), skip_group_check=True),
                            inc=("cT", 1) if dt == ET - 1 else None)
                    grp = self.c["cT"]
                    em.wait("scalar", "cT", grp)
                    em.op("scalar", lambda e, fb=fb, half=half, e_real=e_real: e.activation(
                        hT[:, fb * SCH:fb * SCH + SCH], psE[0][:, half * 512:half * 512 + 512],
                        AF.Relu, bias=b1T[:, fb * 8 + e_real:fb * 8 + e_real + 1]), inc=("cS", 1))
                    evac_cS[(ei, c, fb)] = self.sfence(em, hT[:, fb * SCH + SCH - 1:fb * SCH + SCH])
                ph1_group_cT[(ei, c)] = self.c["cT"]
                lnT_slot_free[c % 2] = ph1_group_cT[(ei, c)]

                if c == NCH - 1 and ei + 1 < NEXP:
                    em.wait("gpsimd", "cT", ph1_group_cT[(ei, c)])
                    w1_ready = emit_w1_load(EORD[ei + 1])

                def emit_w2_group(g, wait_cT):
                    if wait_cT:
                        em.wait("scalar", "cT", wait_cT)
                    src = dram["w2b_d"][e_real, 4 * g * 128:(4 * g + 4) * 128, :].rearrange(
                        "(gg p) d -> p gg d", p=128)
                    return em.op("scalar", lambda e, g=g, src=src: e.dma_start(
                        out=w2ring[:, :].rearrange("p (s d) -> p s d", s=8)[:, (4 * g) % 8:(4 * g) % 8 + 4, :],
                        in_=src), inc=("dwr%d" % g, 16))

                if c == 0:
                    sn = self._pc_dps_snap[e_real]
                    em.wait("scalar", "dps0", sn[0])
                    em.wait("scalar", "dps1", sn[1])
                w2_ld = {0: emit_w2_group(0, w2_free_cT), 1: emit_w2_group(1, w2_free_cT)}

                for fb in range(FB):
                    em.wait("tensor", "cS", evac_cS[(ei, c, fb)])
                    em.wait("tensor", "dwr%d" % (fb // 4), w2_ld[fb // 4])
                    sb_order = [1, 2, 3, 0] if fb == 0 else list(range(NSB))
                    for sbi, sb in enumerate(sb_order):
                        if fb == 0 and is_last and c > 0:
                            em.wait("tensor", "cV", psE_free_cV[sb])
                        if fb == 0 and sb == 0:
                            em.wait("tensor", "cS", evac_cS[(ei, c, FB - 1)])
                        for dh in range(2):
                            fb_last = (sbi == NSB - 1 and dh == 1)
                            em.op("tensor", lambda e, fb=fb, sb=sb, dh=dh: e.matmul(
                                psE[sb][:, dh * 512:dh * 512 + 512],
                                hT[:, fb * SCH + sb * 128: fb * SCH + sb * 128 + 128],
                                w2ring[:, (fb % 8) * 1024 + dh * 512:(fb % 8) * 1024 + dh * 512 + 512],
                                start=(fb == 0), stop=(fb == FB - 1), skip_group_check=True),
                                inc=("cT", 1) if fb_last else None)
                    if fb % 4 == 3 and fb // 4 + 2 < 8:
                        w2_ld[fb // 4 + 2] = emit_w2_group(fb // 4 + 2, self.c["cT"])
                ph2_cT = self.c["cT"]
                w2_free_cT = ph2_cT

                if not is_last:
                    continue

                em.wait("vector", "cT", ph2_cT)
                for sb in range(NSB):
                    psE_free_cV[sb] = em.op("vector", lambda e, sb=sb: e.tensor_copy(
                        eo_bf[:, sb * 1024:sb * 1024 + 1024], psE[sb][:, :]), inc=("cV", 1))
                g0 = c * NSB
                last_sq = None
                for sb in range(NSB):
                    g = g0 + sb
                    if prev_x2_rd is not None:
                        em.wait("gpsimd", "cV", prev_x2_rd)
                    x2ld = em.op("gpsimd", lambda e, g=g: e.dma_start(
                        out=x2t[:, :], in_=dram["x2_d"][g * 128:g * 128 + 128, :]), inc=("dx2t", 16))
                    em.wait("vector", "dx2t", x2ld)
                    v1t = em.op("vector", lambda e, g=g: e.tensor_scalar(
                        tA[:, :], x2t[:, :], st2["rstd"][:, g:g + 1], st2["mrstd"][:, g:g + 1],
                        op0=ALU.mult, op1=ALU.add), inc=("cV", 1))
                    prev_x2_rd = v1t
                    em.wait("vector", "cV", v1t)
                    v2t = em.op("vector", lambda e: e.tensor_add(tA[:, :], tA[:, :], b2rep[:, :]), inc=("cV", 1))
                    em.wait("vector", "cV", v2t)
                    sl = eo_bf[:, sb * 1024:sb * 1024 + 1024]
                    v3t = em.op("vector", lambda e, sl=sl, g=g: e.tensor_add(
                        sl, sl, tA[:, :]), inc=("cV", 1))
                    em.wait("vector", "cV", v3t)
                    vr = em.op("vector", lambda e, sl=sl, g=g: e.reduce_sum(
                        stI["sx"][:, g:g + 1], sl, axis=AX.X), inc=("cV", 1))
                    last_sq = em.op("vector", lambda e, sl=sl, g=g: e.scalar_tensor_tensor(
                        junk_ap(), sl, 1.0, sl, op0=ALU.mult, op1=ALU.mult,
                        accum_out=stI["sq"][:, g:g + 1]), inc=("cV", 1))
                em.wait("vector", "cV", last_sq)
                tinyI = self.ln_chain(em, stI, g0, w=NSB)
                em.wait("vector", "cV", tinyI)
                for sb in range(NSB):
                    g = g0 + sb
                    sl = eo_bf[:, sb * 1024:sb * 1024 + 1024]
                    v4t = em.op("vector", lambda e, sl=sl, g=g: e.tensor_scalar(
                        tA[:, :], sl, stI["rstd"][:, g:g + 1], stI["mrstd"][:, g:g + 1],
                        op0=ALU.mult, op1=ALU.add), inc=("cV", 1))
                    em.wait("vector", "cV", v4t)
                    v5t = em.op("vector", lambda e: e.tensor_mul(tA[:, :], tA[:, :], gamma_rep[:, :]), inc=("cV", 1))
                    em.wait("vector", "cV", v5t)
                    v6t = em.op("vector", lambda e, sl=sl, g=g: e.scalar_tensor_tensor(
                        sl, tA[:, :], 1.0, beta_rep[:, :], op0=ALU.mult, op1=ALU.add,
                        accum_out=stO["sx"][:, g:g + 1]), inc=("cV", 1))
                    em.wait("vector", "cV", v6t)
                    last_sq = em.op("vector", lambda e, sl=sl, g=g: e.scalar_tensor_tensor(
                        junk_ap(), sl, 1.0, sl, op0=ALU.mult, op1=ALU.mult,
                        accum_out=stO["sq"][:, g:g + 1]), inc=("cV", 1))
                em.wait("vector", "cV", last_sq)
                tinyO = self.ln_chain(em, stO, g0, w=NSB)
                em.wait("vector", "cV", tinyO)
                for sb in range(NSB):
                    g = g0 + sb
                    sl = eo_bf[:, sb * 1024:sb * 1024 + 1024]
                    em.wait("gpsimd", "cV", prev_x2_rd)
                    if prev_yst is not None:
                        em.wait("gpsimd", "dy", prev_yst)
                    x2ld = em.op("gpsimd", lambda e, g=g: e.dma_start(
                        out=x2t[:, :], in_=dram["x2_d"][g * 128:g * 128 + 128, :]), inc=("dx2t", 16))
                    if prev_yst is not None:
                        em.wait("vector", "dy", prev_yst)
                    v7t = em.op("vector", lambda e, sl=sl, g=g: e.tensor_scalar(
                        tB[:, :], sl, stO["rstd"][:, g:g + 1], stO["mrstd"][:, g:g + 1],
                        op0=ALU.mult, op1=ALU.add), inc=("cV", 1))
                    em.wait("vector", "dx2t", x2ld)
                    em.wait("vector", "cV", v7t)
                    v8t = em.op("vector", lambda e: e.tensor_add(tB[:, :], tB[:, :], x2t[:, :]), inc=("cV", 1))
                    prev_x2_rd = v8t
                    em.wait("gpsimd", "cV", v8t)
                    yst = em.op("gpsimd", lambda e, g=g: e.dma_start(
                        out=dram["y"][g * 128:g * 128 + 128, :], in_=tB[:, :]), inc=("dy", 16))
                    prev_yst = yst
            if "ht" in self.debug and ei == 0:
                em.wait("sync", "cS", evac_cS[(0, NCH - 1, FB - 1)])
                em.op("sync", lambda e: e.dma_start(out=dram["dbg_ht"], in_=hT[:, :]), inc=("ddbg", 16))
        em.flush(blk)
        self._finish(blk, em)

    def _finish(self, blk, em):
        for eng, sem in (("tensor", "cT"), ("vector", "cV"), ("scalar", "cS"), ("gpsimd", "cG")):
            em.wait(eng, sem, self.c[sem])
            em.op(eng, lambda e: e.nop(), inc=(sem, 1))
        fin = {k: self.c[k] for k in self.c}
        for s_ in fin:
            em.wait("sync", s_, fin[s_])
        em.flush(blk)


def build_nc(debug=(), phase_limit=99, **kw):
    return Builder(debug=debug, phase_limit=phase_limit, **kw).build()


def make_in_map(inputs, core):
    f = lambda a: np.ascontiguousarray(np.asarray(a, dtype=np.float32))
    return {
        "x": f(inputs["L0_attn_x"][core]),
        "wa": f(inputs["L0_attn_W_A"]),
        "wv": f(inputs["L0_attn_W_V"]),
        "wo": f(inputs["L0_attn_W_O"]),
        "bo": f(inputs["L0_attn_b_O"]).reshape(1, E),
        "w1": f(inputs["L0_ffn_w1"]),
        "b1": f(inputs["L0_ffn_b1"]),
        "w2": f(inputs["L0_ffn_w2"]),
        "b2": f(inputs["L0_ffn_b2"]),
        "gamma": f(inputs["L0_ffn_gamma"]).reshape(1, E),
        "beta": f(inputs["L0_ffn_beta"]).reshape(1, E),
    }


def kernel(**inputs):
    from concourse.bass_utils import run_bass_kernel_spmd
    nc = build_nc()
    in_maps = [make_in_map(inputs, c) for c in range(8)]
    res = run_bass_kernel_spmd(nc, in_maps, core_ids=list(range(8)))
    return np.stack([np.asarray(r["y"]) for r in res.results], axis=0)


# version 13
# speedup vs baseline: 1.1089x; 1.1089x over previous
import numpy as np
import dataclasses
from contextlib import ExitStack
import concourse.bass as bass
import concourse.mybir as mybir

F32 = mybir.dt.float32
BF16 = mybir.dt.bfloat16
U8 = mybir.dt.uint8
AF = mybir.ActivationFunctionType
ALU = mybir.AluOpType
AX = mybir.AxisListType

S, E, H, D, FF, NE = 4096, 1024, 16, 64, 4096, 8
ST = 32
ET = 8
FB = 32
SCH = 512
NCH = S // SCH
NSB = SCH // 128
WPADW = 4352
LNEPS = 1e-5
EXPERT_ORDER = [7, 6, 5, 4, 3, 2, 1, 0]


class Arena:

    def __init__(self, nc, ctx, nbytes):
        self.nc = nc
        self.nbytes = nbytes
        self.h = ctx.enter_context(nc.sbuf_tensor("arena", [128, nbytes], U8))
        self.base = nc.lookup_mloc(self.h).addr

    def carve(self, name, off, shape, dtype):
        per_part = int(np.prod(shape[1:])) * mybir.dt.size(dtype)
        assert off + per_part <= self.nbytes, (name, off, per_part)
        return self.nc.alloc_sbuf_tensor_at(name, shape, dtype, offset=self.base + off)


class Em:
    ENGINES = ("tensor", "vector", "scalar", "sync", "gpsimd")

    def __init__(self, B):
        self.B = B
        self.lists = {k: [] for k in self.ENGINES}

    def op(self, eng, fn, inc=None):
        semh = kk = None
        if inc is not None:
            sem, kk = inc
            self.B.c[sem] += kk
            semh = self.B.sems[sem]

        def run(e, fn=fn, semh=semh, kk=kk):
            instr = fn(e)
            if semh is not None:
                instr.then_inc(semh, kk)
        self.lists[eng].append(run)
        return self.B.c[inc[0]] if inc is not None else None

    def wait(self, eng, sem, val):
        if val is None or val <= 0:
            return
        semh = self.B.sems[sem]
        assert val <= self.B.c[sem], ("wait beyond emitted", sem, val, self.B.c[sem])
        self.lists[eng].append(lambda e, s=semh, v=val: e.wait_ge(s, v))

    def flush(self, blk):
        for eng in self.ENGINES:
            lst = self.lists[eng]
            if not lst:
                continue
            getattr(blk, eng)(lambda e, lst=lst: [f(e) for f in lst])
        self.lists = {k: [] for k in self.ENGINES}


class Builder:
    def __init__(self, debug=(), phase_limit=99, conv_heads=H, experts=None):
        self.debug = set(debug)
        self.phase_limit = phase_limit
        self.conv_heads = conv_heads
        self.experts = list(experts) if experts is not None else list(EXPERT_ORDER)
        self.DMA_SEMS = ['dbc', 'dw1i', 'dw2i', 'dwg', 'dx0', 'dx1', 'dso0', 'dso1', 'dwp', 'dz0', 'dz1', 'dfl', 'dl0', 'dl1', 'dw1', 'dwr0', 'dwr1', 'dwr2', 'dwr3', 'dwr4', 'dwr5', 'dwr6', 'dwr7', 'dx2t', 'dy', 'ddbg', 'dpc0', 'dpc1', 'dps0', 'dps1', 'dx2', 'dso2']
        self.c = {k: 0 for k in (tuple(self.DMA_SEMS) + ("cT", "cV", "cS", "cG"))}
        self.sems = {}

    def declare_dram(self, nc):
        d = {}
        d["x"] = nc.dram_tensor("x", [S, E], F32, kind="ExternalInput").ap()
        d["wa"] = nc.dram_tensor("wa", [H, E], F32, kind="ExternalInput").ap()
        d["wv"] = nc.dram_tensor("wv", [E, E], F32, kind="ExternalInput").ap()
        d["wo"] = nc.dram_tensor("wo", [E, E], F32, kind="ExternalInput").ap()
        d["bo"] = nc.dram_tensor("bo", [1, E], F32, kind="ExternalInput").ap()
        d["w1"] = nc.dram_tensor("w1", [NE, E, FF], F32, kind="ExternalInput").ap()
        d["b1"] = nc.dram_tensor("b1", [NE, FF], F32, kind="ExternalInput").ap()
        d["w2"] = nc.dram_tensor("w2", [NE, FF, E], F32, kind="ExternalInput").ap()
        d["b2"] = nc.dram_tensor("b2", [NE, E], F32, kind="ExternalInput").ap()
        d["gamma"] = nc.dram_tensor("gamma", [1, E], F32, kind="ExternalInput").ap()
        d["beta"] = nc.dram_tensor("beta", [1, E], F32, kind="ExternalInput").ap()
        d["y"] = nc.dram_tensor("y", [S, E], F32, kind="ExternalOutput").ap()
        d["wpad_d"] = nc.dram_tensor("wpad_d", [H, WPADW], BF16, kind="Internal").ap()
        d["x2_d"] = nc.dram_tensor("x2_d", [S, E], F32, kind="Internal").ap()
        d["ln2t_d"] = nc.dram_tensor("ln2t_d", [E, S], BF16, kind="Internal").ap()
        d["w1b_d"] = nc.dram_tensor("w1b_d", [NE, E, FF], BF16, kind="Internal").ap()
        d["w2b_d"] = nc.dram_tensor("w2b_d", [NE, FF, E], BF16, kind="Internal").ap()
        dbg_shapes = {"ln1tf": ([128, 8 * 4096], BF16), "zsm": ([H, WPADW], BF16),
                      "vhm": ([128, 16 * 2048], BF16), "outnat": ([128, 32 * 1024], BF16),
                      "outt": ([128, 8 * 4096], BF16), "ht": ([128, FB * SCH], BF16),
                      "x2": ([S, E], F32), "ln2t": ([E, S], BF16),
                      "zt": ([16, 4096], F32), "expsb": ([16, 4096], F32),
                      "smax": ([16, 4], F32), "wat": ([128, ET * 16], BF16),
                      "zst": ([128, ST * 16], F32)}
        extra = {"zt": ["expsb", "smax", "wat", "zst"]}
        names = set(self.debug)
        for n in self.debug:
            names.update(extra.get(n, []))
        for name in sorted(names):
            shape, dt_ = dbg_shapes[name]
            d["dbg_" + name] = nc.dram_tensor("dbg_" + name, shape, dt_, kind="ExternalOutput").ap()
        return d

    def bcast_row(self, em, eng, dsem, dst, src_row_ap, wait=None):
        n = dst.shape[1]
        src = dataclasses.replace(src_row_ap, ap=[[0, 128], [1, n]])
        if wait is not None:
            em.wait(eng, wait[0], wait[1])
        return em.op(eng, lambda e, d=dst, s=src: e.dma_start(out=d[:, :], in_=s),
                     inc=(dsem, 16))

    def sfence(self, em, read_ap):
        em.wait("scalar", "cS", self.c["cS"])
        tok = self.fencetok
        self._fttog ^= 1
        col = self._fttog
        p = read_ap.shape[0]
        return em.op("scalar", lambda e, r=read_ap, p=p, col=col: e.activation(
            tok[0:p, col:col + 1], r, AF.Identity), inc=("cS", 1))

    def ln_chain_s(self, em, stats, col, w=1, eps=LNEPS):
        cc = lambda t: t[:, col:col + w]
        mean, rstd, mrstd = stats["mean"], stats["rstd"], stats["mrstd"]
        tA, tB = stats["tA"], stats["tB"]
        v1 = em.op("vector", lambda e: e.tensor_scalar_mul(cc(mean), cc(stats["sx"]), 1.0 / E), inc=("cV", 1))
        v2 = em.op("vector", lambda e: e.tensor_scalar(cc(tA), cc(stats["sq"]), 1.0 / E, float(eps),
                                                       op0=ALU.mult, op1=ALU.add), inc=("cV", 1))
        em.wait("vector", "cV", v1)
        v3 = em.op("vector", lambda e: e.tensor_mul(cc(tB), cc(mean), cc(mean)), inc=("cV", 1))
        em.wait("vector", "cV", v3)
        v4 = em.op("vector", lambda e: e.tensor_sub(cc(tA), cc(tA), cc(tB)), inc=("cV", 1))
        em.wait("scalar", "cV", v4)
        s1 = em.op("scalar", lambda e: e.activation(cc(tB), cc(tA), AF.Ln), inc=("cS", 1))
        em.wait("scalar", "cS", s1)
        s2 = em.op("scalar", lambda e: e.activation(cc(rstd), cc(tB), AF.Exp, scale=-0.5), inc=("cS", 1))
        f = self.sfence(em, cc(rstd)[:, 0:1])
        em.wait("vector", "cS", f)
        v5a = em.op("vector", lambda e: e.tensor_mul(cc(mrstd), cc(mean), cc(rstd)), inc=("cV", 1))
        em.wait("vector", "cV", v5a)
        v5 = em.op("vector", lambda e: e.tensor_scalar_mul(cc(mrstd), cc(mrstd), -1.0), inc=("cV", 1))
        return v5

    def ln_chain(self, em, stats, col, w=1, eps=LNEPS):
        cc = lambda t: t[:, col:col + w]
        mean, rstd, mrstd = stats["mean"], stats["rstd"], stats["mrstd"]
        tA, tB = stats["tA"], stats["tB"]
        I32 = mybir.dt.int32
        last = None

        def vop(fn):
            nonlocal last
            if last is not None:
                em.wait("vector", "cV", last)
            last = em.op("vector", fn, inc=("cV", 1))
            return last
        vop(lambda e: e.tensor_scalar_mul(cc(mean), cc(stats["sx"]), 1.0 / E))
        vop(lambda e: e.tensor_scalar(cc(tA), cc(stats["sq"]), 1.0 / E, float(eps),
                                      op0=ALU.mult, op1=ALU.add))
        vop(lambda e: e.tensor_mul(cc(tB), cc(mean), cc(mean)))
        vop(lambda e: e.tensor_sub(cc(tA), cc(tA), cc(tB)))
        vop(lambda e: e.tensor_single_scalar(cc(rstd).bitcast(I32), cc(tA).bitcast(I32),
                                             1, ALU.arith_shift_right))
        vop(lambda e: e.tensor_scalar(cc(rstd).bitcast(I32), cc(rstd).bitcast(I32),
                                      -1, 0x5F3759DF, op0=ALU.mult, op1=ALU.add))
        for _ in range(2):
            vop(lambda e: e.tensor_mul(cc(tB), cc(tA), cc(rstd)))
            vop(lambda e: e.tensor_mul(cc(tB), cc(tB), cc(rstd)))
            vop(lambda e: e.tensor_scalar(cc(tB), cc(tB), -0.5, 1.5, op0=ALU.mult, op1=ALU.add))
            vop(lambda e: e.tensor_mul(cc(rstd), cc(rstd), cc(tB)))
        vop(lambda e: e.tensor_mul(cc(mrstd), cc(mean), cc(rstd)))
        v5 = vop(lambda e: e.tensor_scalar_mul(cc(mrstd), cc(mrstd), -1.0))
        return v5

    def build(self):
        nc = bass.Bass("TRN2", target_bir_lowering=False)
        dram = self.declare_dram(nc)
        self.nc = nc
        with ExitStack() as ctx:
            self._build_inner(nc, dram, ctx)
        return nc

    def _build_inner(self, nc, dram, ctx):
        EC = ctx.enter_context
        for k in self.c:
            self.sems[k] = EC(nc.semaphore(k))

        ar = Arena(nc, ctx, 151552)
        S0, S1, S2 = 0, 65536, 86016

        wot = EC(nc.sbuf_tensor("wot", [128, 8 * 1024], BF16))
        junk2 = EC(nc.sbuf_tensor("junk", [128, 2 * 1024], BF16))
        self._jktog = 0
        def junk_ap():
            self._jktog ^= 1
            return junk2[:, self._jktog * 1024:(self._jktog + 1) * 1024]
        zst = EC(nc.sbuf_tensor("zst", [128, ST * 16], F32))
        b1T = EC(nc.sbuf_tensor("b1T", [128, FB * NE], F32))
        wat = EC(nc.sbuf_tensor("wat", [128, ET * 16], BF16))
        ident_bf = EC(nc.sbuf_tensor("ident_bf", [128, 128], BF16))
        jmat_bf = EC(nc.sbuf_tensor("jmat_bf", [128, 128], BF16))
        jmat_f = EC(nc.sbuf_tensor("jmat_f", [128, 128], F32))
        ident_f = EC(nc.sbuf_tensor("ident_f", [128, 128], F32))
        gamma_rep = EC(nc.sbuf_tensor("gamma_rep", [128, 1024], F32))
        beta_rep = EC(nc.sbuf_tensor("beta_rep", [128, 1024], F32))
        b2rep = EC(nc.sbuf_tensor("b2rep", [128, 1024], F32))
        self.fencetok = EC(nc.sbuf_tensor("fencetok", [128, 2], F32))
        self._fttog = 0

        def stat_block(prefix):
            return {k: EC(nc.sbuf_tensor(prefix + k, [128, ST], F32))
                    for k in ("sx", "sq", "mean", "rstd", "mrstd", "tA", "tB")}
        st1 = stat_block("s1_")
        st4 = stat_block("s4_")
        st2 = stat_block("s2_")
        stI = stat_block("sI_")
        stO = stat_block("sO_")
        smax = EC(nc.sbuf_tensor("smax", [16, 4], F32))

        attn_psum = ExitStack()
        pv0 = attn_psum.enter_context(nc.psum_tensor("pv0", [128, 1024], F32))
        pv1 = attn_psum.enter_context(nc.psum_tensor("pv1", [128, 1024], F32))
        pc2 = attn_psum.enter_context(nc.psum_tensor("pc2", [128, 1024], F32))
        psT = attn_psum.enter_context(nc.psum_tensor("psT", [128, 2048], BF16))
        pvs = [pv0, pv1]

        blk = EC(nc.Block())
        em = Em(self)

        def mk_ident(t, anti):
            m = em.op("gpsimd", lambda e, t=t: e.memset(t[:, :], 0.0), inc=("cG", 1))
            em.wait("gpsimd", "cG", m)
            return em.op("gpsimd", lambda e, t=t, anti=anti: e.affine_select(
                out=t[:, :], in_=t[:, :], compare_op=ALU.not_equal,
                fill=1.0, base=(-127 if anti else 0),
                pattern=[[1 if anti else -1, 128]], channel_multiplier=1), inc=("cG", 1))
        gid_ibf = mk_ident(ident_bf, False)
        gid_jbf = mk_ident(jmat_bf, True)
        em.wait("vector", "cG", gid_ibf)
        vc_if = em.op("vector", lambda e: e.tensor_copy(ident_f[:, :], ident_bf[:, :]), inc=("cV", 1))
        em.wait("vector", "cG", gid_jbf)
        vc_jf = em.op("vector", lambda e: e.tensor_copy(jmat_f[:, :], jmat_bf[:, :]), inc=("cV", 1))
        gid = gid_jbf

        self.bcast_row(em, "sync", "dbc", gamma_rep, dram["gamma"])
        self.bcast_row(em, "sync", "dbc", beta_rep, dram["beta"])
        self.bcast_row(em, "sync", "dbc", b2rep, dram["b2"][0:1, :])

        b1_sb = ar.carve("b1_sb", S0, [8, FF], F32)
        wa_sb = ar.carve("wa_sb", S0 + 16384, [16, E], F32)
        wa_bf = ar.carve("wa_bf", S0 + 20480, [16, E], BF16)
        wnat = ar.carve("wnat", S0 + 24576, [128, 8 * 1024], BF16)
        wvt = ar.carve("wvt", S1, [128, 8 * 1024], BF16)

        d1 = em.op("sync", lambda e: e.dma_start(out=b1_sb[:, :], in_=dram["b1"]), inc=("dw1i", 16))
        d2 = em.op("sync", lambda e: e.dma_start(out=wa_sb[:, :], in_=dram["wa"]), inc=("dw2i", 16))
        wv_src = dram["wv"].rearrange("(t p) e -> p t e", p=128)
        d3 = em.op("gpsimd", lambda e: e.dma_start(
            out=wnat[:, :].rearrange("p (t e) -> p t e", t=8), in_=wv_src), inc=("dwg", 16))

        def emit_wT(dst, src_dG, prev_psT_reader=None):
            evs = {}
            cnt = 0
            em.wait("tensor", "dwg", src_dG)
            em.wait("tensor", "cG", gid_ibf)
            if prev_psT_reader is not None:
                em.wait("tensor", "cV", prev_psT_reader)
            for t_ in range(8):
                for q in range(ET):
                    half = cnt % 2
                    if cnt >= 2:
                        em.wait("tensor", "cV", evs[cnt - 2])
                    tt = em.op("tensor", lambda e, t_=t_, q=q, half=half: e.transpose(
                        psT[:, half * 1024:half * 1024 + 128],
                        wnat[:, t_ * 1024 + q * 128: t_ * 1024 + q * 128 + 128],
                        ident_bf[:, :]), inc=("cT", 1))
                    em.wait("vector", "cT", tt)
                    evs[cnt] = em.op("vector", lambda e, t_=t_, q=q, half=half, dst=dst: e.tensor_copy(
                        dst[:, q * 1024 + t_ * 128: q * 1024 + t_ * 128 + 128],
                        psT[:, half * 1024:half * 1024 + 128]), inc=("cV", 1))
                    cnt += 1
            return evs[cnt - 1]

        wv_done = emit_wT(wvt, d3)
        em.wait("gpsimd", "cV", wv_done)
        wo_src = dram["wo"].rearrange("(t p) e -> p t e", p=128)
        d4 = em.op("gpsimd", lambda e: e.dma_start(
            out=wnat[:, :].rearrange("p (t e) -> p t e", t=8), in_=wo_src), inc=("dwg", 16))
        init_done_v = emit_wT(wot, d4)
        em.wait("vector", "dw2i", d2)
        wacast = em.op("vector", lambda e: e.tensor_copy(wa_bf[:, :], wa_sb[:, :]), inc=("cV", 1))
        em.wait("tensor", "cV", wacast)
        em.wait("tensor", "cV", init_done_v)
        wa_ev = {}
        for q in range(ET):
            half = q % 2
            if q >= 2:
                em.wait("tensor", "cV", wa_ev[q - 2])
            t = em.op("tensor", lambda e, q=q, half=half: e.transpose(
                psT[:, half * 1024:half * 1024 + 16],
                wa_bf[0:16, q * 128:q * 128 + 128], ident_bf[0:16, 0:16]), inc=("cT", 1))
            em.wait("vector", "cT", t)
            wa_ev[q] = em.op("vector", lambda e, q=q, half=half: e.tensor_copy(
                wat[:, q * 16:q * 16 + 16], psT[:, half * 1024:half * 1024 + 16]), inc=("cV", 1))

        em.wait("tensor", "dw1i", d1)
        em.wait("tensor", "cV", vc_if)
        b1ev = {}
        for fb in range(FB):
            pv = pvs[fb % 2]
            if fb >= 2:
                em.wait("tensor", "cV", b1ev[fb - 2])
            t = em.op("tensor", lambda e, fb=fb, pv=pv: e.transpose(
                pv[:, 0:8], b1_sb[0:8, fb * 128:fb * 128 + 128], ident_f[0:8, 0:8]),
                inc=("cT", 1))
            em.wait("vector", "cT", t)
            b1ev[fb] = em.op("vector", lambda e, fb=fb, pv=pv: e.tensor_copy(
                b1T[:, fb * 8:fb * 8 + 8], pv[:, 0:8]), inc=("cV", 1))

        em.flush(blk)

        if self.phase_limit < 1:
            self._finish(blk, em)
            return

        xb = ar.carve("xb", S2, [128, 2 * 1024], F32)
        ln1bf = ar.carve("ln1bf", S2 + 8192, [128, 2 * 1024], BF16)
        ln1tf = ar.carve("ln1tf", S0, [128, 8 * 4096], BF16)

        red_done, sq_done = {}, {}
        for stt in range(ST):
            buf = stt % 2
            if stt >= 2:
                em.wait("sync", "cV", red_done[stt - 2])
                em.wait("sync", "cS", sq_done[stt - 2])
            ld = em.op("sync", lambda e, stt=stt, buf=buf: e.dma_start(
                out=xb[:, buf * 1024:buf * 1024 + 1024],
                in_=dram["x"][stt * 128:stt * 128 + 128, :]), inc=("dx%d" % buf, 16))
            em.wait("vector", "dx%d" % buf, ld)
            red_done[stt] = em.op("vector", lambda e, stt=stt, buf=buf: e.reduce_sum(
                st1["sx"][:, stt:stt + 1], xb[:, buf * 1024:buf * 1024 + 1024], axis=AX.X), inc=("cV", 1))
            em.wait("scalar", "dx%d" % buf, ld)
            sq_done[stt] = em.op("scalar", lambda e, stt=stt, buf=buf: e.activation(
                junk_ap(), xb[:, buf * 1024:buf * 1024 + 1024], AF.Square,
                accum_out=st1["sq"][:, stt:stt + 1]), inc=("cS", 1))
        fsq = self.sfence(em, st1["sq"][:, ST - 1:ST])
        em.wait("vector", "cV", red_done[ST - 1])
        em.wait("vector", "cS", fsq)
        v1 = em.op("vector", lambda e: e.tensor_scalar_mul(st1["mean"][:, :], st1["sx"][:, :], 1.0 / E), inc=("cV", 1))
        v2 = em.op("vector", lambda e: e.tensor_scalar(st1["tA"][:, :], st1["sq"][:, :], 1.0 / E, float(LNEPS),
                                                       op0=ALU.mult, op1=ALU.add), inc=("cV", 1))
        em.wait("vector", "cV", v2)
        v3 = em.op("vector", lambda e: e.tensor_mul(st1["tB"][:, :], st1["mean"][:, :], st1["mean"][:, :]), inc=("cV", 1))
        em.wait("vector", "cV", v3)
        v4 = em.op("vector", lambda e: e.tensor_sub(st1["tA"][:, :], st1["tA"][:, :], st1["tB"][:, :]), inc=("cV", 1))
        em.wait("scalar", "cV", v4)
        s1_ = em.op("scalar", lambda e: e.activation(st1["tB"][:, :], st1["tA"][:, :], AF.Ln), inc=("cS", 1))
        em.wait("scalar", "cS", s1_)
        s2_ = em.op("scalar", lambda e: e.activation(st1["rstd"][:, :], st1["tB"][:, :], AF.Exp, scale=-0.5), inc=("cS", 1))
        f1_ = self.sfence(em, st1["rstd"][:, ST - 1:ST])
        em.wait("vector", "cS", f1_)
        ln1tiny0 = em.op("vector", lambda e: e.tensor_mul(st1["mrstd"][:, :], st1["mean"][:, :], st1["rstd"][:, :]), inc=("cV", 1))
        em.wait("vector", "cV", ln1tiny0)
        ln1tiny = em.op("vector", lambda e: e.tensor_scalar_mul(st1["mrstd"][:, :], st1["mrstd"][:, :], -1.0), inc=("cV", 1))

        ap_done, tr1_done, ev1c = {}, {}, {}
        for stt in range(ST):
            buf = stt % 2
            half = stt % 2
            if stt >= 2:
                em.wait("sync", "cV", ap_done[stt - 2])
            else:
                em.wait("sync", "cV", red_done[ST - 2 + stt])
                em.wait("sync", "cS", sq_done[ST - 2 + stt])
            ld = em.op("sync", lambda e, stt=stt, buf=buf: e.dma_start(
                out=xb[:, buf * 1024:buf * 1024 + 1024],
                in_=dram["x"][stt * 128:stt * 128 + 128, :]), inc=("dx%d" % buf, 16))
            em.wait("vector", "dx%d" % buf, ld)
            em.wait("vector", "cV", ln1tiny)
            if stt >= 2:
                em.wait("vector", "cT", tr1_done[stt - 2])
            ap_done[stt] = em.op("vector", lambda e, stt=stt, buf=buf: e.tensor_scalar(
                ln1bf[:, buf * 1024:buf * 1024 + 1024], xb[:, buf * 1024:buf * 1024 + 1024],
                st1["rstd"][:, stt:stt + 1], st1["mrstd"][:, stt:stt + 1],
                op0=ALU.mult, op1=ALU.add), inc=("cV", 1))
            em.wait("tensor", "cV", ap_done[stt])
            if stt == 0:
                em.wait("tensor", "cG", gid_jbf)
            if stt >= 2:
                em.wait("tensor", "cV", ev1c[stt - 2])
            t = None
            for q in range(ET):
                t = em.op("tensor", lambda e, q=q, buf=buf, half=half: e.transpose(
                    psT[:, half * 1024 + q * 128: half * 1024 + q * 128 + 128],
                    ln1bf[:, buf * 1024 + q * 128: buf * 1024 + q * 128 + 128],
                    jmat_bf[:, :]), inc=("cT", 1))
            tr1_done[stt] = t
            em.wait("vector", "cT", t)
            ev1c[stt] = em.op("vector", lambda e, stt=stt, half=half: e.tensor_copy(
                ln1tf[:, :].rearrange("p (q s) -> p q s", q=ET)[:, :, stt * 128:stt * 128 + 128],
                psT[:, half * 1024:half * 1024 + 1024].rearrange("p (q m) -> p q m", q=ET)), inc=("cV", 1))
        em.flush(blk)

        if "ln1tf" in self.debug:
            em.wait("sync", "cV", ev1c[ST - 1])
            em.op("sync", lambda e: e.dma_start(out=dram["dbg_ln1tf"], in_=ln1tf[:, :]), inc=("ddbg", 16))
            em.flush(blk)
        if self.phase_limit < 2:
            self._finish(blk, em)
            return

        vhm = ar.carve("vhm", S2, [128, H * 2 * 1024], BF16)
        vev, zev = {}, {}
        for stt in range(ST):
            pv = pvs[stt % 2]
            zoff = (stt % 2) * 512
            em.wait("tensor", "cV", ev1c[stt])
            em.wait("tensor", "cV", init_done_v)
            if stt >= 2:
                em.wait("tensor", "cV", vev[stt - 2])
                em.wait("tensor", "cV", zev[stt - 2])
            for q in range(ET):
                lhs = ln1tf[:, q * 4096 + stt * 128: q * 4096 + stt * 128 + 128]
                em.op("tensor", lambda e, q=q, pv=pv, lhs=lhs: e.matmul(
                    pv[:, 0:512], lhs, wvt[:, q * 1024:q * 1024 + 512],
                    start=(q == 0), stop=(q == ET - 1), skip_group_check=True))
                em.op("tensor", lambda e, q=q, pv=pv, lhs=lhs: e.matmul(
                    pv[:, 512:1024], lhs, wvt[:, q * 1024 + 512:q * 1024 + 1024],
                    start=(q == 0), stop=(q == ET - 1), skip_group_check=True))
                em.op("tensor", lambda e, q=q, zoff=zoff, lhs=lhs: e.matmul(
                    pc2[:, zoff:zoff + 16], lhs, wat[:, q * 16:q * 16 + 16],
                    start=(q == 0), stop=(q == ET - 1), skip_group_check=True),
                    inc=("cT", 1) if q == ET - 1 else None)
            grp = self.c["cT"]
            em.wait("vector", "cT", grp)
            vev[stt] = em.op("vector", lambda e, stt=stt, pv=pv: e.tensor_copy(
                vhm[:, :].rearrange("p (h c d) -> p h c d", h=H, d=D)[:, :, stt, :],
                pv[:, :].rearrange("p (h d) -> p h d", h=H)), inc=("cV", 1))
            zev[stt] = em.op("vector", lambda e, stt=stt, zoff=zoff: e.tensor_copy(
                zst[:, stt * 16:stt * 16 + 16], pc2[:, zoff:zoff + 16]), inc=("cV", 1))
        ph2_all_cT = self.c["cT"]
        em.flush(blk)

        zT = ar.carve("zT", S0, [16, 4096], F32)
        exp_sb = ar.carve("exp_sb", S0 + 16384, [16, 4096], F32)
        wpad_sb = ar.carve("wpad_sb", S0 + 32768, [16, WPADW], BF16)

        em.wait("gpsimd", "cT", ph2_all_cT)
        wp0 = em.op("gpsimd", lambda e: e.memset(wpad_sb[:, :], 0.0), inc=("cG", 1))
        ztev = {}
        em.wait("tensor", "cV", zev[ST - 1])
        em.wait("tensor", "cV", vc_jf)
        for stt in range(ST):
            pv = pvs[stt % 2]
            if stt >= 2:
                em.wait("tensor", "cV", ztev[stt - 2])
            t = em.op("tensor", lambda e, stt=stt, pv=pv: e.transpose(
                pv[0:16, 0:128], zst[:, stt * 16:stt * 16 + 16], jmat_f[:, :]), inc=("cT", 1))
            em.wait("vector", "cT", t)
            ztev[stt] = em.op("vector", lambda e, stt=stt, pv=pv: e.tensor_copy(
                zT[0:16, stt * 128:stt * 128 + 128], pv[0:16, 0:128]), inc=("cV", 1))
        em.wait("vector", "cV", ztev[ST - 1])
        r1 = em.op("vector", lambda e: e.reduce_max(smax[0:16, 0:1], zT[0:16, :], axis=AX.X, negate=True), inc=("cV", 1))
        em.wait("vector", "cV", r1)
        r2 = em.op("vector", lambda e: e.tensor_scalar_mul(smax[0:16, 1:2], smax[0:16, 0:1], 0.125), inc=("cV", 1))
        em.wait("scalar", "cV", r2)
        r3 = em.op("scalar", lambda e: e.activation(
            exp_sb[0:16, :], zT[0:16, :], AF.Exp, scale=0.125, bias=smax[0:16, 1:2]), inc=("cS", 1))
        em.wait("scalar", "cS", r3)
        r3f = em.op("scalar", lambda e: e.activation(
            smax[0:16, 2:3], exp_sb[0:16, 4095:4096], AF.Identity), inc=("cS", 1))
        em.wait("vector", "cS", r3f)
        r3b = em.op("vector", lambda e: e.reduce_sum(smax[0:16, 2:3], exp_sb[0:16, :], axis=AX.X), inc=("cV", 1))
        em.wait("vector", "cV", r3b)
        r4 = em.op("vector", lambda e: e.reciprocal(smax[0:16, 3:4], smax[0:16, 2:3]), inc=("cV", 1))
        em.wait("vector", "cV", r4)
        em.wait("vector", "cG", wp0)
        r5 = em.op("vector", lambda e: e.tensor_scalar(
            wpad_sb[0:16, 127:127 + 4096], exp_sb[0:16, :], smax[0:16, 3:4], None,
            op0=ALU.mult), inc=("cV", 1))
        em.wait("sync", "cV", r5)
        wpD = em.op("sync", lambda e: e.dma_start(out=dram["wpad_d"], in_=wpad_sb[:, :]), inc=("dwp", 16))
        em.flush(blk)

        if "zsm" in self.debug:
            em.wait("sync", "dwp", wpD)
            em.op("sync", lambda e: e.dma_start(out=dram["dbg_zsm"], in_=wpad_sb[:, :]), inc=("ddbg", 16))
            em.flush(blk)
        if "zt" in self.debug:
            em.wait("sync", "dwp", wpD)
            em.op("sync", lambda e: e.dma_start(out=dram["dbg_zt"], in_=zT[0:16, :]), inc=("ddbg", 16))
            em.op("sync", lambda e: e.dma_start(out=dram["dbg_expsb"], in_=exp_sb[0:16, :]), inc=("ddbg", 16))
            em.op("sync", lambda e: e.dma_start(out=dram["dbg_smax"], in_=smax[0:16, :]), inc=("ddbg", 16))
            em.op("sync", lambda e: e.dma_start(out=dram["dbg_wat"], in_=wat[:, :]), inc=("ddbg", 16))
            em.op("sync", lambda e: e.dma_start(out=dram["dbg_zst"], in_=zst[:, :]), inc=("ddbg", 16))
            em.flush(blk)
        if self.phase_limit < 3:
            self._finish(blk, em)
            return

        Zbuf = ar.carve("Zbuf", S1, [128, 2 * 4096], BF16)
        out_nat = ar.carve("out_nat", S0, [128, ST * 1024], BF16)
        rot = [pv0, pv1, pc2]
        zld, zconsumed, cv_evac = {}, {}, {}

        def emit_zbuild(h):
            buf = h % 2
            em.wait("scalar", "dwp", wpD)
            if h >= 2:
                em.wait("scalar", "cT", zconsumed[h - 2])
            src = dataclasses.replace(dram["wpad_d"], offset=h * WPADW, ap=[[1, 128], [1, 4096]])
            zld[h] = em.op("scalar", lambda e, buf=buf, src=src: e.dma_start(
                out=Zbuf[:, buf * 4096:buf * 4096 + 4096], in_=src), inc=("dz%d" % buf, 16))

        NHC = self.conv_heads
        emit_zbuild(0)
        if NHC > 1:
            emit_zbuild(1)
        rix = 0
        for h in range(NHC):
            buf = h % 2
            em.wait("tensor", "dz%d" % (h % 2), zld[h])
            em.wait("tensor", "cV", ztev[ST - 1])
            em.wait("tensor", "cV", vev[ST - 1])
            for half in range(2):
                ps = rot[rix % 3]
                if rix >= 3:
                    em.wait("tensor", "cV", cv_evac[rix - 3])
                base = 1024 * half
                ks = range(16) if half == 0 else range(32)
                grp = None
                for k in ks:
                    lo = max(64 * k, base)
                    hi = base + 1024
                    cur = lo
                    while cur < hi:
                        nxt = min(hi, (cur // 512 + 1) * 512)
                        a, b = cur, nxt
                        cur = nxt
                        bank = a // 512
                        klast = 8 * bank + 7
                        stop = (k == min(klast, ks[-1]))
                        is_final = (k == ks[-1] and b == hi)
                        grp = em.op("tensor", lambda e, a=a, b=b, k=k, ps=ps, base=base, h=h, buf=buf, stop=stop: e.matmul(
                            ps[:, a - base:b - base],
                            Zbuf[:, buf * 4096 + k * 128: buf * 4096 + k * 128 + 128],
                            vhm[:, h * 2048 + a - 64 * k: h * 2048 + b - 64 * k],
                            start=(k == 0), stop=stop, skip_group_check=True),
                            inc=("cT", 1) if is_final else None)
                if half == 1:
                    zconsumed[h] = grp
                    if h + 2 < NHC:
                        emit_zbuild(h + 2)
                em.wait("vector", "cT", grp)
                cv_evac[rix] = em.op("vector", lambda e, h=h, half=half, ps=ps: e.tensor_copy(
                    out_nat[:, :].rearrange("p (c f) -> p c f", f=1024)[:, 16 * half:16 * half + 16, h * 64:h * 64 + 64],
                    ps[:, :].rearrange("p (c d) -> p c d", d=D)), inc=("cV", 1))
                rix += 1
        conv_done = self.c["cV"]
        em.flush(blk)

        if "outnat" in self.debug:
            em.wait("sync", "cV", conv_done)
            em.op("sync", lambda e: e.dma_start(out=dram["dbg_outnat"], in_=out_nat[:, :]), inc=("ddbg", 16))
            em.flush(blk)
        if "vhm" in self.debug:
            em.wait("sync", "cV", conv_done)
            em.op("sync", lambda e: e.dma_start(out=dram["dbg_vhm"], in_=vhm[:, :]), inc=("ddbg", 16))
            em.flush(blk)
        if self.phase_limit < 4:
            self._finish(blk, em)
            return

        outT = ar.carve("outT", S2, [128, 8 * 4096], BF16)
        ev3c = {}
        for stt in range(ST):
            half = stt % 2
            em.wait("tensor", "cV", conv_done)
            if stt >= 2:
                em.wait("tensor", "cV", ev3c[stt - 2])
            for q in range(ET):
                em.op("tensor", lambda e, stt=stt, q=q, half=half: e.transpose(
                    psT[:, half * 1024 + q * 128: half * 1024 + q * 128 + 128],
                    out_nat[:, stt * 1024 + q * 128: stt * 1024 + q * 128 + 128],
                    ident_bf[:, :]), inc=("cT", 1) if q == ET - 1 else None)
            em.wait("vector", "cT", self.c["cT"])
            ev3c[stt] = em.op("vector", lambda e, stt=stt, half=half: e.tensor_copy(
                outT[:, :].rearrange("p (q s) -> p q s", q=ET)[:, :, stt * 128:stt * 128 + 128],
                psT[:, half * 1024:half * 1024 + 1024].rearrange("p (q m) -> p q m", q=ET)), inc=("cV", 1))
        em.flush(blk)
        if "outt" in self.debug:
            em.wait("sync", "cV", ev3c[ST - 1])
            em.op("sync", lambda e: e.dma_start(out=dram["dbg_outt"], in_=outT[:, :]), inc=("ddbg", 16))
            em.flush(blk)

        pstg_f = ar.carve("pstg_f", S1, [128, 2 * 1024], F32)
        pstg_b = ar.carve("pstg_b", S1 + 8192, [128, 2 * 1024], BF16)
        self._pc_i = 0
        self._pc_cast = {}
        self._pc_st = {}
        self._pc_dps_snap = {}
        self._pc_gate = ("cV", conv_done)

        def emit_precast_pieces(eidx, npieces):
            for _ in range(npieces):
                i = self._pc_i
                if i >= 64:
                    return
                if i == 0 and self._pc_gate is not None:
                    em.wait("gpsimd", self._pc_gate[0], self._pc_gate[1])
                    self._pc_gate = None
                self._pc_i = i + 1
                p = i % 2
                if i < 32:
                    dt, qq = i // 4, i % 4
                    src = dram["w1"][eidx, dt * 128:dt * 128 + 128, qq * 1024:qq * 1024 + 1024]
                    dst = dram["w1b_d"][eidx, dt * 128:dt * 128 + 128, qq * 1024:qq * 1024 + 1024]
                else:
                    rr = i - 32
                    src = dram["w2"][eidx, rr * 128:rr * 128 + 128, :]
                    dst = dram["w2b_d"][eidx, rr * 128:rr * 128 + 128, :]
                if i >= 2:
                    em.wait("gpsimd", "cG", self._pc_cast[i - 2])
                ld = em.op("gpsimd", lambda e, src=src, p=p: e.dma_start(
                    out=pstg_f[:, p * 1024:p * 1024 + 1024], in_=src), inc=("dpc%d" % p, 16))
                em.wait("gpsimd", "dpc%d" % p, ld)
                if i >= 2:
                    em.wait("gpsimd", "dps%d" % p, self._pc_st[i - 2])
                self._pc_cast[i] = em.op("gpsimd", lambda e, p=p: e.tensor_copy(
                    pstg_b[:, p * 1024:p * 1024 + 1024], pstg_f[:, p * 1024:p * 1024 + 1024]),
                    inc=("cG", 1))
                em.wait("gpsimd", "cG", self._pc_cast[i])
                self._pc_st[i] = em.op("gpsimd", lambda e, dst=dst, p=p: e.dma_start(
                    out=dst, in_=pstg_b[:, p * 1024:p * 1024 + 1024]), inc=("dps%d" % p, 16))

        def finish_precast(eidx):
            emit_precast_pieces(eidx, 64)
            self._pc_i = 0
            self._pc_dps_snap[eidx] = (self.c["dps0"], self.c["dps1"])

        lnstg = ar.carve("lnstg", S0, [128, ET * 512], BF16)
        xb4 = ar.carve("xb4", S0 + 8192, [128, 3 * 1024], F32)
        x2b = ar.carve("x2b", S0 + 20480, [128, 3 * 1024], F32)
        osb = ar.carve("osb", S0 + 32768, [128, 3 * 1024], F32)
        alb = ar.carve("alb", S0 + 45056, [128, 2 * 1024], F32)
        ln2bf = ar.carve("ln2bf", S0 + 53248, [128, 3 * 1024], BF16)
        bo_rep = ar.carve("bo_rep", S0 + 59392, [128, 1024], F32)

        out_nat_free = ev3c[ST - 1]
        bo_ld = self.bcast_row(em, "sync", "dbc", bo_rep, dram["bo"], wait=("cV", out_nat_free))
        oev, x2st, x2done, lnf, rs4_, rs2_ = {}, {}, {}, {}, {}, {}
        al_of, lnap_of = {}, {}
        stgflush = None
        lnevac = None
        pvs3 = [pv0, pv1, pc2]
        for stt in range(ST):
            buf = stt % 3
            ab = stt % 2
            half = stt % 2
            pv = pvs3[stt % 3]
            if self.phase_limit >= 5 and stt % 2 == 1:
                emit_precast_pieces(self.experts[0], 4)
            if stt >= 3:
                em.wait("sync", "cV", x2done[stt - 3])
            xld = em.op("sync", lambda e, stt=stt, buf=buf: e.dma_start(
                out=xb4[:, buf * 1024:buf * 1024 + 1024],
                in_=dram["x"][stt * 128:stt * 128 + 128, :]), inc=("dx%d" % buf, 16))
            em.wait("tensor", "cV", ev3c[stt])
            if stt >= 3:
                em.wait("tensor", "cV", oev[stt - 3])
            for q in range(ET):
                lhs = outT[:, q * 4096 + stt * 128: q * 4096 + stt * 128 + 128]
                em.op("tensor", lambda e, pv=pv, lhs=lhs, q=q: e.matmul(
                    pv[:, 0:512], lhs, wot[:, q * 1024:q * 1024 + 512],
                    start=(q == 0), stop=(q == ET - 1), skip_group_check=True))
                em.op("tensor", lambda e, pv=pv, lhs=lhs, q=q: e.matmul(
                    pv[:, 512:1024], lhs, wot[:, q * 1024 + 512:q * 1024 + 1024],
                    start=(q == 0), stop=(q == ET - 1), skip_group_check=True),
                    inc=("cT", 1) if q == ET - 1 else None)
            em.wait("vector", "cT", self.c["cT"])
            em.wait("vector", "dbc", bo_ld)
            oev[stt] = em.op("vector", lambda e, stt=stt, buf=buf, pv=pv: e.scalar_tensor_tensor(
                osb[:, buf * 1024:buf * 1024 + 1024], bo_rep[:, :], 1.0, pv[:, :],
                op0=ALU.mult, op1=ALU.add,
                accum_out=st4["sx"][:, stt:stt + 1]), inc=("cV", 1))
            em.wait("vector", "cV", oev[stt])
            rs4_[stt] = em.op("vector", lambda e, stt=stt, buf=buf: e.scalar_tensor_tensor(
                junk_ap(), osb[:, buf * 1024:buf * 1024 + 1024], 1.0,
                osb[:, buf * 1024:buf * 1024 + 1024], op0=ALU.mult, op1=ALU.mult,
                accum_out=st4["sq"][:, stt:stt + 1]), inc=("cV", 1))
            em.wait("vector", "cV", rs4_[stt])
            tiny4 = self.ln_chain(em, st4, stt)
            em.wait("vector", "cV", tiny4)
            al = em.op("vector", lambda e, stt=stt, buf=buf, ab=ab: e.tensor_scalar(
                alb[:, ab * 1024:ab * 1024 + 1024], osb[:, buf * 1024:buf * 1024 + 1024],
                st4["rstd"][:, stt:stt + 1], st4["mrstd"][:, stt:stt + 1],
                op0=ALU.mult, op1=ALU.add), inc=("cV", 1))
            em.wait("vector", "dx%d" % buf, xld)
            em.wait("vector", "cV", al)
            if stt >= 3:
                em.wait("vector", "dso%d" % buf, x2st[stt - 3])
            x2c = em.op("vector", lambda e, stt=stt, buf=buf, ab=ab: e.scalar_tensor_tensor(
                x2b[:, buf * 1024:buf * 1024 + 1024], alb[:, ab * 1024:ab * 1024 + 1024], 1.0,
                xb4[:, buf * 1024:buf * 1024 + 1024],
                op0=ALU.mult, op1=ALU.add,
                accum_out=st2["sx"][:, stt:stt + 1]), inc=("cV", 1))
            x2done[stt] = x2c
            em.wait("sync", "cV", x2c)
            x2st[stt] = em.op("sync", lambda e, stt=stt, buf=buf: e.dma_start(
                out=dram["x2_d"][stt * 128:stt * 128 + 128, :],
                in_=x2b[:, buf * 1024:buf * 1024 + 1024]), inc=("dso%d" % buf, 16))
            em.wait("vector", "cV", x2c)
            rs2_[stt] = em.op("vector", lambda e, stt=stt, buf=buf: e.scalar_tensor_tensor(
                junk_ap(), x2b[:, buf * 1024:buf * 1024 + 1024], 1.0,
                x2b[:, buf * 1024:buf * 1024 + 1024], op0=ALU.mult, op1=ALU.mult,
                accum_out=st2["sq"][:, stt:stt + 1]), inc=("cV", 1))
            em.wait("vector", "cV", rs2_[stt])
            tiny2 = self.ln_chain(em, st2, stt)
            em.wait("vector", "cV", tiny2)
            if stt >= 3:
                em.wait("vector", "cT", lnf[stt - 3])
            lnap = em.op("vector", lambda e, stt=stt, buf=buf: e.tensor_scalar(
                ln2bf[:, buf * 1024:buf * 1024 + 1024], x2b[:, buf * 1024:buf * 1024 + 1024],
                st2["rstd"][:, stt:stt + 1], st2["mrstd"][:, stt:stt + 1],
                op0=ALU.mult, op1=ALU.add), inc=("cV", 1))
            em.wait("tensor", "cV", lnap)
            if stt >= 2:
                em.wait("tensor", "cV", lnevacs[stt - 2])
            t = None
            for q in range(ET):
                t = em.op("tensor", lambda e, q=q, buf=buf, half=half: e.transpose(
                    psT[:, half * 1024 + q * 128: half * 1024 + q * 128 + 128],
                    ln2bf[:, buf * 1024 + q * 128: buf * 1024 + q * 128 + 128],
                    ident_bf[:, :]), inc=("cT", 1) if q == ET - 1 else None)
            lnf[stt] = self.c["cT"]
            em.wait("vector", "cT", self.c["cT"])
            if stt % 4 == 0 and stt >= 4:
                em.wait("vector", "dfl", stgflush)
            if stt == 0:
                lnevacs = {}
            lnevacs[stt] = em.op("vector", lambda e, stt=stt, half=half: e.tensor_copy(
                lnstg[:, :].rearrange("p (q s) -> p q s", q=ET)[:, :, (stt % 4) * 128:(stt % 4) * 128 + 128],
                psT[:, half * 1024:half * 1024 + 1024].rearrange("p (q m) -> p q m", q=ET)), inc=("cV", 1))
            if stt % 4 == 3:
                grpi = stt // 4
                em.wait("scalar", "cV", lnevacs[stt])
                for qh in range(2):
                    dst = dataclasses.replace(
                        dram["ln2t_d"], offset=qh * 4 * 128 * S + grpi * 512,
                        ap=[[S, 128], [128 * S, 4], [1, 512]])
                    stgflush = em.op("scalar", lambda e, dst=dst, qh=qh: e.dma_start(
                        out=dst,
                        in_=lnstg[:, :].rearrange("p (q s) -> p q s", q=ET)[:, qh * 4:qh * 4 + 4, :]),
                        inc=("dfl", 16))
        if self.phase_limit >= 5:
            finish_precast(self.experts[0])
        ph4 = {k: self.c[k] for k in self.c}
        em.flush(blk)

        if "x2" in self.debug:
            em.wait("sync", "dso0", ph4["dso0"])
            em.wait("sync", "dso1", ph4["dso1"])
            em.op("sync", lambda e: e.dma_start(out=dram["dbg_x2"], in_=dram["x2_d"]), inc=("ddbg", 16))
            em.flush(blk)
        if "ln2t" in self.debug:
            em.wait("sync", "dfl", ph4["dfl"])
            em.op("sync", lambda e: e.dma_start(out=dram["dbg_ln2t"], in_=dram["ln2t_d"]), inc=("ddbg", 16))
            em.flush(blk)
        if self.phase_limit < 5:
            self._finish(blk, em)
            return

        attn_psum.close()
        p5 = ExitStack()
        psE = [p5.enter_context(nc.psum_tensor(f"psE{i}", [128, 1024], F32)) for i in range(4)]
        ctx.callback(p5.close)

        w1sb = ar.carve("w1sb", S0, [128, ET * 4096], BF16)
        hT = ar.carve("hT", S2, [128, FB * SCH], BF16)
        lnTb = ar.carve("lnTb", S2 + 32768, [128, 2 * ET * SCH], BF16)
        w2ring = ar.carve("w2ring", S2 + 49152, [128, 8 * 1024], BF16)
        eo_bf = ar.carve("eo_bf", S1, [128, 4 * 1024], BF16)
        x2t = ar.carve("x2t", S1 + 8192, [128, 1024], F32)
        tA = ar.carve("tA", S1 + 12288, [128, 1024], F32)
        tB = ar.carve("tB", S1 + 16384, [128, 1024], F32)

        for eng in ("tensor", "vector", "scalar", "sync", "gpsimd"):
            for s_ in (["cT", "cV", "cS"] + self.DMA_SEMS):
                em.wait(eng, s_, ph4[s_])

        def emit_w1_load(eidx):
            sn = self._pc_dps_snap[eidx]
            em.wait("sync", "dps0", sn[0])
            em.wait("sync", "dps1", sn[1])
            last = None
            for dt in range(ET):
                src = dram["w1b_d"][eidx, dt * 128:dt * 128 + 128, :]
                last = em.op("sync", lambda e, dt=dt, src=src: e.dma_start(
                    out=w1sb[:, dt * 4096:dt * 4096 + 4096], in_=src), inc=("dw1", 16))
            return last

        EORD = self.experts
        NEXP = len(EORD)
        w1_ready = emit_w1_load(EORD[0])

        def emit_lnT_load(c, slot, wait_cT):
            if wait_cT:
                em.wait("sync", "cT", wait_cT)
            last = None
            for qh in range(2):
                src = dataclasses.replace(
                    dram["ln2t_d"], offset=qh * 4 * 128 * S + c * SCH,
                    ap=[[S, 128], [128 * S, 4], [1, SCH]])
                last = em.op("sync", lambda e, src=src, slot=slot, qh=qh: e.dma_start(
                    out=lnTb[:, :].rearrange("p (b q s) -> p b q s", b=2, q=ET)[:, slot, qh * 4:qh * 4 + 4, :],
                    in_=src), inc=("dl%d" % slot, 16))
            return last

        ph1_group_cT = {}
        evac_cS = {}
        psE_free_cV = {i: 0 for i in range(4)}
        prev_x2_rd = None
        prev_yst = None
        lnT_slot_free = {0: None, 1: None}
        w2_free_cT = None
        lnT_ready = {0: emit_lnT_load(0, 0, None)}

        for ei, e_real in enumerate(EORD):
            is_last = (e_real == 0)
            for c in range(NCH):
                if ei + 1 < NEXP:
                    emit_precast_pieces(EORD[ei + 1], 8)
                    if c == NCH - 1:
                        finish_precast(EORD[ei + 1])
                nxt = c + 1
                if nxt < NCH:
                    lnT_ready[nxt] = emit_lnT_load(nxt, nxt % 2, lnT_slot_free[nxt % 2])
                elif ei + 1 < NEXP:
                    lnT_ready[0] = emit_lnT_load(0, 0, lnT_slot_free[0])

                em.wait("tensor", "dl%d" % (c % 2), lnT_ready[c])
                if c == 0:
                    em.wait("tensor", "dw1", w1_ready)
                slot = c % 2
                for fb in range(FB):
                    half = fb % 2
                    if fb >= 2:
                        em.wait("tensor", "cS", evac_cS[(ei, c, fb - 2)])
                    elif is_last and c > 0:
                        em.wait("tensor", "cV", psE_free_cV[0])
                    rhs = lnTb[:, :].rearrange("p (b q s) -> p b q s", b=2, q=ET)
                    for dt in range(ET):
                        em.op("tensor", lambda e, fb=fb, dt=dt, half=half, slot=slot, rhs=rhs: e.matmul(
                            psE[0][:, half * 512:half * 512 + 512],
                            w1sb[:, dt * 4096 + fb * 128: dt * 4096 + fb * 128 + 128],
                            rhs[:, slot, dt, :],
                            start=(dt == 0), stop=(dt == ET - 1), skip_group_check=True),
                            inc=("cT", 1) if dt == ET - 1 else None)
                    grp = self.c["cT"]
                    em.wait("scalar", "cT", grp)
                    em.op("scalar", lambda e, fb=fb, half=half, e_real=e_real: e.activation(
                        hT[:, fb * SCH:fb * SCH + SCH], psE[0][:, half * 512:half * 512 + 512],
                        AF.Relu, bias=b1T[:, fb * 8 + e_real:fb * 8 + e_real + 1]), inc=("cS", 1))
                    evac_cS[(ei, c, fb)] = self.sfence(em, hT[:, fb * SCH + SCH - 1:fb * SCH + SCH])
                ph1_group_cT[(ei, c)] = self.c["cT"]
                lnT_slot_free[c % 2] = ph1_group_cT[(ei, c)]

                if c == NCH - 1 and ei + 1 < NEXP:
                    em.wait("gpsimd", "cT", ph1_group_cT[(ei, c)])
                    w1_ready = emit_w1_load(EORD[ei + 1])

                def emit_w2_group(g, wait_cT):
                    if wait_cT:
                        em.wait("scalar", "cT", wait_cT)
                    src = dram["w2b_d"][e_real, 4 * g * 128:(4 * g + 4) * 128, :].rearrange(
                        "(gg p) d -> p gg d", p=128)
                    return em.op("scalar", lambda e, g=g, src=src: e.dma_start(
                        out=w2ring[:, :].rearrange("p (s d) -> p s d", s=8)[:, (4 * g) % 8:(4 * g) % 8 + 4, :],
                        in_=src), inc=("dwr%d" % g, 16))

                if c == 0:
                    sn = self._pc_dps_snap[e_real]
                    em.wait("scalar", "dps0", sn[0])
                    em.wait("scalar", "dps1", sn[1])
                w2_ld = {0: emit_w2_group(0, w2_free_cT), 1: emit_w2_group(1, w2_free_cT)}

                for fb in range(FB):
                    em.wait("tensor", "cS", evac_cS[(ei, c, fb)])
                    em.wait("tensor", "dwr%d" % (fb // 4), w2_ld[fb // 4])
                    sb_order = [1, 2, 3, 0] if fb == 0 else list(range(NSB))
                    for sbi, sb in enumerate(sb_order):
                        if fb == 0 and is_last and c > 0:
                            em.wait("tensor", "cV", psE_free_cV[sb])
                        if fb == 0 and sb == 0:
                            em.wait("tensor", "cS", evac_cS[(ei, c, FB - 1)])
                        for dh in range(2):
                            fb_last = (sbi == NSB - 1 and dh == 1)
                            em.op("tensor", lambda e, fb=fb, sb=sb, dh=dh: e.matmul(
                                psE[sb][:, dh * 512:dh * 512 + 512],
                                hT[:, fb * SCH + sb * 128: fb * SCH + sb * 128 + 128],
                                w2ring[:, (fb % 8) * 1024 + dh * 512:(fb % 8) * 1024 + dh * 512 + 512],
                                start=(fb == 0), stop=(fb == FB - 1), skip_group_check=True),
                                inc=("cT", 1) if fb_last else None)
                    if fb % 4 == 3 and fb // 4 + 2 < 8:
                        w2_ld[fb // 4 + 2] = emit_w2_group(fb // 4 + 2, self.c["cT"])
                ph2_cT = self.c["cT"]
                w2_free_cT = ph2_cT

                if not is_last:
                    continue

                em.wait("vector", "cT", ph2_cT)
                for sb in range(NSB):
                    psE_free_cV[sb] = em.op("vector", lambda e, sb=sb: e.tensor_copy(
                        eo_bf[:, sb * 1024:sb * 1024 + 1024], psE[sb][:, :]), inc=("cV", 1))
                g0 = c * NSB
                last_sq = None
                for sb in range(NSB):
                    g = g0 + sb
                    if prev_x2_rd is not None:
                        em.wait("gpsimd", "cV", prev_x2_rd)
                    x2ld = em.op("gpsimd", lambda e, g=g: e.dma_start(
                        out=x2t[:, :], in_=dram["x2_d"][g * 128:g * 128 + 128, :]), inc=("dx2t", 16))
                    em.wait("vector", "dx2t", x2ld)
                    v1t = em.op("vector", lambda e, g=g: e.tensor_scalar(
                        tA[:, :], x2t[:, :], st2["rstd"][:, g:g + 1], st2["mrstd"][:, g:g + 1],
                        op0=ALU.mult, op1=ALU.add), inc=("cV", 1))
                    prev_x2_rd = v1t
                    em.wait("vector", "cV", v1t)
                    v2t = em.op("vector", lambda e: e.tensor_add(tA[:, :], tA[:, :], b2rep[:, :]), inc=("cV", 1))
                    em.wait("vector", "cV", v2t)
                    sl = eo_bf[:, sb * 1024:sb * 1024 + 1024]
                    v3t = em.op("vector", lambda e, sl=sl, g=g: e.tensor_add(
                        sl, sl, tA[:, :]), inc=("cV", 1))
                    em.wait("vector", "cV", v3t)
                    vr = em.op("vector", lambda e, sl=sl, g=g: e.reduce_sum(
                        stI["sx"][:, g:g + 1], sl, axis=AX.X), inc=("cV", 1))
                    last_sq = em.op("vector", lambda e, sl=sl, g=g: e.scalar_tensor_tensor(
                        junk_ap(), sl, 1.0, sl, op0=ALU.mult, op1=ALU.mult,
                        accum_out=stI["sq"][:, g:g + 1]), inc=("cV", 1))
                em.wait("vector", "cV", last_sq)
                tinyI = self.ln_chain(em, stI, g0, w=NSB)
                em.wait("vector", "cV", tinyI)
                for sb in range(NSB):
                    g = g0 + sb
                    sl = eo_bf[:, sb * 1024:sb * 1024 + 1024]
                    v4t = em.op("vector", lambda e, sl=sl, g=g: e.tensor_scalar(
                        tA[:, :], sl, stI["rstd"][:, g:g + 1], stI["mrstd"][:, g:g + 1],
                        op0=ALU.mult, op1=ALU.add), inc=("cV", 1))
                    em.wait("vector", "cV", v4t)
                    v5t = em.op("vector", lambda e: e.tensor_mul(tA[:, :], tA[:, :], gamma_rep[:, :]), inc=("cV", 1))
                    em.wait("vector", "cV", v5t)
                    v6t = em.op("vector", lambda e, sl=sl, g=g: e.scalar_tensor_tensor(
                        sl, tA[:, :], 1.0, beta_rep[:, :], op0=ALU.mult, op1=ALU.add,
                        accum_out=stO["sx"][:, g:g + 1]), inc=("cV", 1))
                    em.wait("vector", "cV", v6t)
                    last_sq = em.op("vector", lambda e, sl=sl, g=g: e.scalar_tensor_tensor(
                        junk_ap(), sl, 1.0, sl, op0=ALU.mult, op1=ALU.mult,
                        accum_out=stO["sq"][:, g:g + 1]), inc=("cV", 1))
                em.wait("vector", "cV", last_sq)
                tinyO = self.ln_chain(em, stO, g0, w=NSB)
                em.wait("vector", "cV", tinyO)
                for sb in range(NSB):
                    g = g0 + sb
                    sl = eo_bf[:, sb * 1024:sb * 1024 + 1024]
                    em.wait("gpsimd", "cV", prev_x2_rd)
                    if prev_yst is not None:
                        em.wait("gpsimd", "dy", prev_yst)
                    x2ld = em.op("gpsimd", lambda e, g=g: e.dma_start(
                        out=x2t[:, :], in_=dram["x2_d"][g * 128:g * 128 + 128, :]), inc=("dx2t", 16))
                    if prev_yst is not None:
                        em.wait("vector", "dy", prev_yst)
                    v7t = em.op("vector", lambda e, sl=sl, g=g: e.tensor_scalar(
                        tB[:, :], sl, stO["rstd"][:, g:g + 1], stO["mrstd"][:, g:g + 1],
                        op0=ALU.mult, op1=ALU.add), inc=("cV", 1))
                    em.wait("vector", "dx2t", x2ld)
                    em.wait("vector", "cV", v7t)
                    v8t = em.op("vector", lambda e: e.tensor_add(tB[:, :], tB[:, :], x2t[:, :]), inc=("cV", 1))
                    prev_x2_rd = v8t
                    em.wait("gpsimd", "cV", v8t)
                    yst = em.op("gpsimd", lambda e, g=g: e.dma_start(
                        out=dram["y"][g * 128:g * 128 + 128, :], in_=tB[:, :]), inc=("dy", 16))
                    prev_yst = yst
            if "ht" in self.debug and ei == 0:
                em.wait("sync", "cS", evac_cS[(0, NCH - 1, FB - 1)])
                em.op("sync", lambda e: e.dma_start(out=dram["dbg_ht"], in_=hT[:, :]), inc=("ddbg", 16))
        em.flush(blk)
        self._finish(blk, em)

    def _finish(self, blk, em):
        for eng, sem in (("tensor", "cT"), ("vector", "cV"), ("scalar", "cS"), ("gpsimd", "cG")):
            em.wait(eng, sem, self.c[sem])
            em.op(eng, lambda e: e.nop(), inc=(sem, 1))
        fin = {k: self.c[k] for k in self.c}
        for s_ in fin:
            em.wait("sync", s_, fin[s_])
        em.flush(blk)


def build_nc(debug=(), phase_limit=99, **kw):
    return Builder(debug=debug, phase_limit=phase_limit, **kw).build()


def make_in_map(inputs, core):
    f = lambda a: np.ascontiguousarray(np.asarray(a, dtype=np.float32))
    return {
        "x": f(inputs["L0_attn_x"][core]),
        "wa": f(inputs["L0_attn_W_A"]),
        "wv": f(inputs["L0_attn_W_V"]),
        "wo": f(inputs["L0_attn_W_O"]),
        "bo": f(inputs["L0_attn_b_O"]).reshape(1, E),
        "w1": f(inputs["L0_ffn_w1"]),
        "b1": f(inputs["L0_ffn_b1"]),
        "w2": f(inputs["L0_ffn_w2"]),
        "b2": f(inputs["L0_ffn_b2"]),
        "gamma": f(inputs["L0_ffn_gamma"]).reshape(1, E),
        "beta": f(inputs["L0_ffn_beta"]).reshape(1, E),
    }


def kernel(**inputs):
    from concourse.bass_utils import run_bass_kernel_spmd
    nc = build_nc()
    in_maps = [make_in_map(inputs, c) for c in range(8)]
    res = run_bass_kernel_spmd(nc, in_maps, core_ids=list(range(8)))
    return np.stack([np.asarray(r["y"]) for r in res.results], axis=0)
